# version 05
# baseline (speedup 1.0000x reference)
import numpy as np
import ml_dtypes
from contextlib import ExitStack

import concourse.bass as bass
import concourse.mybir as mybir
from concourse.bass_utils import run_bass_kernel_spmd

F32 = mybir.dt.float32
BF16 = mybir.dt.bfloat16
AF = mybir.ActivationFunctionType
ALU = mybir.AluOpType
AX = mybir.AxisListType

B = 16
SEQ = 2048
D = 1024
H = 16
HD = 64
NCORES = 8
BPC = B // NCORES
TPC = BPC * SEQ
NT = TPC // 128
NTS = SEQ // 128
NC = D // 128
NEG = -1.0e30

bfd = ml_dtypes.bfloat16


def _host_consts():
    c = {}
    c["IDENT_BF"] = np.eye(128, dtype=bfd)
    c["ONES_BF"] = np.ones((1, 128), dtype=bfd)
    c["ONES_F32"] = np.ones((1, 128), dtype=np.float32)
    c["EIGHTH_F32"] = np.full((1, 128), 0.125, dtype=np.float32)
    ql = np.arange(128)[:, None]
    j = np.arange(512)[None, :]
    qmk = ql + 384 - j
    diag = np.zeros((17, 128, 512), np.float32)
    for i in range(16):
        diag[i] = (qmk == i)
    diag[16] = (qmk >= 16)
    c["AA_DIAG"] = np.ascontiguousarray(diag.reshape(17, 128 * 512))
    c["AA_MASK"] = np.where(qmk < 0, NEG, 0.0).astype(np.float32)

    half = HD // 2
    i32 = np.arange(half, dtype=np.float64)
    theta = 1.0 / 10000.0 ** (2.0 * i32 / HD)
    t = np.arange(SEQ, dtype=np.float64)[:, None]
    ang = t * theta[None, :]
    cosv = np.cos(ang)
    sinv = np.sin(ang)
    ct = np.zeros((SEQ, HD)); st = np.zeros((SEQ, HD))
    ct[:, 0::2] = cosv; ct[:, 1::2] = cosv
    st[:, 0::2] = -sinv; st[:, 1::2] = sinv
    c["ROPE_COS"] = np.ascontiguousarray(np.tile(ct, (1, H)).astype(bfd))
    c["ROPE_SIN"] = np.ascontiguousarray(np.tile(st, (1, H)).astype(bfd))

    WINDOW = 256
    half2 = np.arange(0, HD, 2, dtype=np.float64)
    inv_freq = 1.0 / 10000.0 ** (half2 / HD)
    tt = np.arange(SEQ, dtype=np.float64)
    fr = tt[:, None] * inv_freq[None, :]
    fr = np.concatenate([fr, fr], -1)
    base_scale = (half2 + 0.4 * HD) / (1.4 * HD)
    power = (tt - SEQ // 2) / (WINDOW // 2)
    sc = base_scale[None, :] ** power[:, None]
    sc = np.concatenate([sc, sc], -1)
    cosr = np.cos(fr); sinr = np.sin(fr)
    sgn = np.concatenate([-np.ones(32), np.ones(32)])
    c["XPOS_CQ"] = (cosr * sc).astype(np.float32)
    c["XPOS_SQ"] = (sinr * sc * sgn[None, :]).astype(np.float32)
    c["XPOS_CK"] = (cosr / sc).astype(np.float32)
    c["XPOS_SK"] = (sinr / sc * sgn[None, :]).astype(np.float32)

    for sub in (0, 1):
        i_loc = np.arange(128)[:, None] + 128 * sub
        jj = np.arange(512)[None, :] - 256
        dmat = i_loc - jj
        valid = (dmat >= 0) & (dmat <= 256)
        c[f"LOC_MASK{sub}"] = np.where(valid, 0.0, NEG).astype(np.float32)
        validf = valid & (jj >= 0)
        c[f"LOC_MASKF{sub}"] = np.where(validf, 0.0, NEG).astype(np.float32)
    return c


_CONSTS = _host_consts()

ENGINES = ("tensor", "vector", "scalar", "gpsimd", "sync")


class Prog:
    STRICT = True

    def __init__(self, nc, es):
        self.nc = nc
        self.es = es
        self.scopes = [es]
        self.streams = {e: [] for e in ENGINES}
        self.s = {}
        self.c = {}
        self.hw = {}

    _uid = 0

    def _uniq(self, name):
        Prog._uid += 1
        return f"{name}_u{Prog._uid}"

    def sb(self, name, shape, dtype):
        return self.scopes[-1].enter_context(
            self.nc.sbuf_tensor(self._uniq(name), list(shape), dtype))

    def ps(self, name, shape, dtype):
        return self.scopes[-1].enter_context(
            self.nc.psum_tensor(self._uniq(name), list(shape), dtype))

    def scope(self):
        self.scopes.append(ExitStack())

    def end_scope(self):
        self.scopes.pop().close()

    def sem(self, name):
        if name not in self.s:
            self.s[name] = self.es.enter_context(self.nc.semaphore(name))
            self.c[name] = 0
        return self.s[name]

    _last_kind = None
    STRICT_ENGINES = ("vector", "scalar", "gpsimd")

    def _self_order(self, eng):
        if self.STRICT and eng in self.STRICT_ENGINES:
            own = self.ENG_SEM[eng]
            if self.at(own) > 0:
                self.wait(eng, own)

    def add(self, eng, fn, selfsem=True):
        if self._last_kind is None:
            self._last_kind = {}
        if selfsem and eng in self.STRICT_ENGINES:
            self._self_order(eng)
            own = self.sem(self.ENG_SEM[eng])
            self.c[self.ENG_SEM[eng]] += 1
            self._last_kind[eng] = "op"
            w = self._flush_pend(eng, True)
            if w is None:
                self.streams[eng].append(
                    lambda h, fn=fn, own=own: fn(h).then_inc(own, 1))
            else:
                self.streams[eng].append(
                    lambda h, fn=fn, own=own, w=w:
                    fn(h).then_inc(own, 1)._wait_ge(w[0], w[1]))
            return
        self._last_kind[eng] = "op"
        w = self._flush_pend(eng, True)
        if w is None:
            self.streams[eng].append(fn)
        else:
            self.streams[eng].append(
                lambda h, fn=fn, w=w: fn(h)._wait_ge(w[0], w[1]))

    def inc(self, eng, semname, fn, n=1, selfsem=True):
        if self._last_kind is None:
            self._last_kind = {}
        ownname = self.ENG_SEM.get(eng)
        strict = selfsem and eng in self.STRICT_ENGINES
        if strict:
            self._self_order(eng)
        sem = self.sem(semname)
        self.c[semname] += n
        self._last_kind[eng] = "op"
        w = self._flush_pend(eng, True)
        if strict and semname != ownname and n == 1:
            own = self.sem(ownname)
            self.c[ownname] += 1
            if w is None:
                self.streams[eng].append(
                    lambda h, fn=fn, sem=sem, own=own:
                    fn(h).then_inc(sem, 1).then_inc(own, 1))
            else:
                self.streams[eng].append(
                    lambda h, fn=fn, sem=sem, own=own, w=w:
                    fn(h).then_inc(sem, 1).then_inc(own, 1)._wait_ge(w[0], w[1]))
        else:
            if w is None:
                self.streams[eng].append(
                    lambda h, fn=fn, sem=sem, n=n: fn(h).then_inc(sem, n))
            else:
                self.streams[eng].append(
                    lambda h, fn=fn, sem=sem, n=n, w=w:
                    fn(h).then_inc(sem, n)._wait_ge(w[0], w[1]))

    def dma(self, eng, semname, out, in_, **kw):
        self.inc(eng, semname,
                 lambda h, out=out, in_=in_, kw=kw: h.dma_start(out=out, in_=in_, **kw),
                 16, selfsem=False)

    def wait(self, eng, semname, cnt=None):
        if self._last_kind is None:
            self._last_kind = {}
        if not hasattr(self, "pend"):
            self.pend = {e: [] for e in ENGINES}
        sem = self.sem(semname)
        v = self.c[semname] if cnt is None else cnt
        key = (eng, semname)
        if v > 0 and v > self.hw.get(key, 0):
            self.hw[key] = v
            self._last_kind[eng] = "wait"
            self.pend[eng].append((sem, v))

    def _flush_pend(self, eng, keep_one):
        if not hasattr(self, "pend"):
            self.pend = {e: [] for e in ENGINES}
        pl = self.pend[eng]
        n = len(pl) - (1 if keep_one and pl else 0)
        for (sem, v) in pl[:n]:
            self.streams[eng].append(lambda h, sem=sem, v=v: h.wait_ge(sem, v))
        rest = pl[n:]
        self.pend[eng] = []
        return rest[0] if rest else None

    def at(self, semname):
        self.sem(semname)
        return self.c[semname]

    def pe_mark(self):
        sem = self.sem("pe")
        self.c["pe"] += 1
        last = self.streams["tensor"].pop()
        self.streams["tensor"].append(
            lambda h, last=last, sem=sem: last(h).then_inc(sem, 1))

    ENG_SEM = {"tensor": "pe", "vector": "dve", "scalar": "act", "gpsimd": "pool"}

    def mark(self, eng):
        semname = self.ENG_SEM[eng]
        sem = self.sem(semname)
        self.c[semname] += 1
        if self.streams[eng] and self._last_kind.get(eng) == "op":
            last = self.streams[eng].pop()
            self.streams[eng].append(
                lambda h, last=last, sem=sem: last(h).then_inc(sem, 1))
        else:
            self.streams[eng].append(
                lambda h, sem=sem: h.nop().then_inc(sem, 1))

    def barrier(self, producers=("tensor", "vector", "scalar"),
                consumers=("tensor", "vector", "scalar", "gpsimd", "sync")):
        for e in producers:
            if self.streams[e]:
                self.mark(e)
        for ce in consumers:
            for e in producers:
                if self.streams[e]:
                    self.wait(ce, self.ENG_SEM[e])

    def finalize(self, block):
        for eng in ENGINES:
            self._flush_pend(eng, False)
        for eng in ENGINES:
            fns = self.streams[eng]
            if not fns:
                continue

            def runner(h, fns=fns):
                for f in fns:
                    f(h)
            getattr(block, eng)(runner)


def colap(t, ncols, col, bcast=1, base_row=0, nrows=128):
    return bass.AP(t, col + base_row * ncols,
                   [[ncols, nrows], [0, bcast]] if bcast > 1 else [[ncols, nrows], [1, 1]])


def ln_stats_tensors(p, tag, n):
    return dict(
        sum=p.sb(f"lnsum{tag}", (128, n), F32),
        ssq=p.sb(f"lnssq{tag}", (128, n), F32),
        mean=p.sb(f"lnmean{tag}", (128, n), F32),
        var=p.sb(f"lnvar{tag}", (128, n), F32),
        rstd=p.sb(f"lnrstd{tag}", (128, n), F32),
        nmr=p.sb(f"lnnmr{tag}", (128, n), F32),
    )


def emit_ln_stats(p, x_dram, st, xbufs, scratch, n_tiles, row_base=0):
    hist = []
    for i in range(n_tiles):
        buf = xbufs[i % 2]
        if i >= 2:
            p.wait("sync", "dve", hist[i - 2][0])
            p.wait("sync", "act", hist[i - 2][1])
        else:
            p.wait("sync", "dve")
            p.wait("sync", "act")
        off = row_base + i * 128
        lane = f"ldx{i % 2}"
        p.dma("sync", lane, buf[:], x_dram[off:off + 128, :])
        p.wait("vector", lane)
        p.inc("vector", "dve",
              lambda v, b=buf, i=i: v.reduce_sum(st["sum"][:, i:i + 1], b[:], AX.X))
        p.wait("scalar", lane)
        p.inc("scalar", "act",
              lambda s, b=buf, i=i: s.activation(scratch[:], b[:], AF.Square,
                                                 accum_out=st["ssq"][:, i:i + 1]))
        hist.append((p.at("dve"), p.at("act")))
    n = n_tiles
    p.wait("vector", "act")
    p.add("vector", lambda v: v.tensor_scalar_mul(st["mean"][:, :n], st["sum"][:, :n], 1.0 / D))
    p.add("vector", lambda v: v.tensor_mul(st["var"][:, :n], st["mean"][:, :n], st["mean"][:, :n]))
    p.add("vector",
          lambda v: v.scalar_tensor_tensor(st["var"][:, :n], st["ssq"][:, :n], 1.0 / D,
                                           st["var"][:, :n], ALU.mult, ALU.subtract))
    p.inc("vector", "dve", lambda v: v.tensor_scalar_add(st["var"][:, :n], st["var"][:, :n], 1e-5))
    p.wait("scalar", "dve")
    p.inc("scalar", "act",
          lambda s: s.activation(st["rstd"][:, :n], st["var"][:, :n], AF.Sqrt))
    p.wait("vector", "act")
    p.add("vector", lambda v: v.reciprocal(st["rstd"][:, :n], st["rstd"][:, :n]))
    p.add("vector", lambda v: v.tensor_mul(st["nmr"][:, :n], st["mean"][:, :n], st["rstd"][:, :n]))
    p.inc("vector", "dve", lambda v: v.tensor_scalar_mul(st["nmr"][:, :n], st["nmr"][:, :n], -1.0))


def emit_norm_transpose(p, x_dram, st, dst_fmT, xbufs, xn_bf, idb, pstr,
                        n_tiles, row_base=0, seq_cols=SEQ):
    pe_done = 0
    dve_done = 0
    act_hist = []
    p.wait("tensor", "ldc")
    for i in range(n_tiles):
        buf = xbufs[i % 2]
        if i >= 2:
            p.wait("sync", "act", act_hist[i - 2])
        else:
            p.wait("sync", "dve")
            p.wait("sync", "act")
        lane = f"ldx{i % 2}"
        p.dma("sync", lane, buf[:], x_dram[row_base + i * 128: row_base + (i + 1) * 128, :])
        p.wait("scalar", lane)
        if i >= 1:
            p.wait("scalar", "pe", pe_done)
        p.inc("scalar", "act",
              lambda s, b=buf, i=i: s.activation(xn_bf[:], b[:], AF.Identity,
                                                 bias=colap(st["nmr"], n_tiles, i),
                                                 scale=colap(st["rstd"], n_tiles, i)))
        act_hist.append(p.at("act"))
        p.wait("tensor", "act")
        if i >= 1:
            p.wait("tensor", "dve", dve_done)
        for cc in range(NC):
            p.add("tensor", lambda t, cc=cc: t.transpose(
                pstr[:, cc * 128:(cc + 1) * 128], xn_bf[:, cc * 128:(cc + 1) * 128], idb[:]))
        p.pe_mark()
        pe_done = p.at("pe")
        p.wait("vector", "pe")
        dst3 = dst_fmT.ap().rearrange("q (c t) -> q c t", c=NC)
        p.inc("vector", "dve",
              lambda v, i=i, dst3=dst3: v.tensor_copy(
                  dst3[:, :, i * 128:(i + 1) * 128], pstr[:, :]))
        dve_done = p.at("dve")


def emit_fm_proj(p, w_sb, w_cols, src_fmT, dst, pq, seq_cols=SEQ, n_fc=NC,
                 evac=None):
    ntg = seq_cols // 512
    pend = p.at("dve")
    evac_done = {0: pend, 1: pend}
    for fc in range(n_fc):
        for tg in range(ntg):
            bank = (fc * ntg + tg) % 2
            pb = pq[bank]
            if evac_done[bank]:
                p.wait("tensor", "dve", evac_done[bank])
            for c in range(NC):
                p.add("tensor", lambda t, c=c, fc=fc, tg=tg, pb=pb: t.matmul(
                    pb[:, :],
                    w_sb[:, c * w_cols + fc * 128: c * w_cols + fc * 128 + 128],
                    src_fmT[:, c * seq_cols + tg * 512: c * seq_cols + tg * 512 + 512],
                    start=(c == 0), stop=(c == NC - 1)))
            p.pe_mark()
            p.wait("vector", "pe")
            if evac is None:
                p.inc("vector", "dve", lambda v, pb=pb, fc=fc, tg=tg: v.tensor_copy(
                    dst[:, fc * seq_cols + tg * 512: fc * seq_cols + tg * 512 + 512],
                    pb[:, :]))
            else:
                evac(p, pb, fc, tg)
            evac_done[bank] = p.at("dve")




def hard_barrier(p):
    for eng in ENGINES:
        for semname in list(p.s.keys()):
            p.wait(eng, semname)


def emit_precast(p, pairs, tag="pc"):
    p.scope()
    CMAX = 8192
    bi = [p.sb(f"{tag}i0", (128, CMAX), F32), p.sb(f"{tag}i1", (128, CMAX), F32)]
    bo = [p.sb(f"{tag}o0", (128, CMAX), BF16), p.sb(f"{tag}o1", (128, CMAX), BF16)]
    lanes = (f"{tag}l0", f"{tag}l1")
    stlanes = (f"{tag}s0", f"{tag}s1")
    t = 0
    pool_hist = {}
    st_hist = {}
    tiles = []
    for (src, dst, R, C) in pairs:
        assert R % 128 == 0
        if C > CMAX:
            assert C % CMAX == 0
            for r0 in range(0, R, 128):
                for c0 in range(0, C, CMAX):
                    tiles.append((src[r0:r0 + 128, c0:c0 + CMAX],
                                  dst[r0:r0 + 128, c0:c0 + CMAX], 1, CMAX))
            continue
        k = max(1, CMAX // C)
        for r0 in range(0, R, 128 * k):
            kk = min(k, (R - r0) // 128)
            tiles.append((src[r0:r0 + 128 * kk, :], dst[r0:r0 + 128 * kk, :],
                          kk, C))
    for (src, dst, kk, C) in tiles:
        w = kk * C
        par = t % 2
        if t >= 2:
            p.wait("sync", "pool", pool_hist[t - 2])
        if kk > 1:
            p.dma("sync", lanes[par],
                  bi[par].ap()[:, 0:w].rearrange("q (k c) -> q k c", k=kk),
                  src.rearrange("(k q) c -> q k c", q=128))
        else:
            p.dma("sync", lanes[par], bi[par][:, 0:C], src)
        p.wait("gpsimd", lanes[par])
        if t >= 2:
            p.wait("gpsimd", stlanes[par], st_hist[t - 2])
        p.inc("gpsimd", "pool", lambda g, par=par, w=w: g.tensor_copy(
            bo[par][:, 0:w], bi[par][:, 0:w]))
        pool_hist[t] = p.at("pool")
        p.wait("gpsimd", "pool")
        if kk > 1:
            p.dma("gpsimd", stlanes[par], dst.rearrange("(k q) c -> q k c", q=128),
                  bo[par].ap()[:, 0:w].rearrange("q (k c) -> q k c", k=kk))
        else:
            p.dma("gpsimd", stlanes[par], dst, bo[par][:, 0:C])
        st_hist[t] = p.at(stlanes[par])
        t += 1
    hard_barrier(p)
    p.end_scope()


def _ffn_weight_relayout(inputs, layers):
    DI = 2730
    lay = 6
    wqkv = np.asarray(inputs["L0_ffn_Wqkv"], np.float32).reshape(lay, NC, 128, 3 * D)
    wqkvr = np.ascontiguousarray(
        wqkv[:layers].transpose(0, 2, 1, 3).reshape(layers * 128, NC * 3 * D))
    wo = np.asarray(inputs["L0_ffn_Wo"], np.float32).reshape(lay, NC, 128, D)
    wor = np.ascontiguousarray(
        wo[:layers].transpose(0, 2, 1, 3).reshape(layers * 128, NC * D))
    w1 = np.asarray(inputs["L0_ffn_W1"], np.float32).reshape(lay, NC, 128, 2 * DI)
    w1r = np.zeros((layers * 44 * 128, NC * 128), np.float32)
    for l in range(layers):
        for j in range(22):
            wj = min(128, DI - j * 128)
            for h in range(2):
                off = h * DI + j * 128
                blk = np.zeros((128, NC, 128), np.float32)
                blk[:, :, :wj] = w1[l, :, :, off:off + wj].transpose(1, 0, 2)
                b = (l * 44 + 2 * j + h) * 128
                w1r[b:b + 128] = blk.reshape(128, NC * 128)
    R2 = ((layers * DI + 127) // 128) * 128
    w2 = np.asarray(inputs["L0_ffn_W2"], np.float32).reshape(lay * DI, D)[:layers * DI]
    w2r = np.zeros((R2, D), np.float32)
    w2r[:layers * DI] = w2
    return (np.ascontiguousarray(wqkvr), np.ascontiguousarray(wor),
            np.ascontiguousarray(w1r), np.ascontiguousarray(w2r))


def _mk_dram(nc, pref):
    def mk(n, shape, dt=F32, kind="ExternalInput"):
        return nc.dram_tensor(pref + n, list(shape), dt, kind=kind).ap()
    return mk

def _dram_attn_a(nc, pref=""):
    mk = _mk_dram(nc, pref)
    return dict(wq=mk("wq", [D, D]), wk=mk("wk", [D, D]), wv=mk("wv", [D, D]),
                wo=mk("wo", [D, D]), bo=mk("bo", [1, D]), rel=mk("rel", [1, 17]),
                adiag=mk("adiag", [17, 128 * 512]), amask=mk("amask", [128, 512]),
                identb=mk("identb", [128, 128], BF16),
                onesb=mk("onesb", [1, 128], BF16), eighth=mk("eighth", [1, 128]))


def build_attn_a(seqs=BPC, qt=NTS, mega=None):
    if mega is None:
        nc = bass.Bass("TRN2", target_bir_lowering=False)
        es = ExitStack()
        p = Prog(nc, es)
        dd = _dram_attn_a(nc)
        x_d = nc.dram_tensor("x", [TPC, D], F32, kind="ExternalInput").ap()
        xo_d = nc.dram_tensor("xo", [TPC, D], F32, kind="ExternalOutput").ap()
    else:
        p, dd, x_d, xo_d = mega
        nc = p.nc
    wq_d = dd["wq"]; wk_d = dd["wk"]; wv_d = dd["wv"]; wo_d = dd["wo"]
    bo_d = dd["bo"]; rel_d = dd["rel"]; diag_d = dd["adiag"]; amask_d = dd["amask"]
    idb_d = dd["identb"]; onesb_d = dd["onesb"]; eighth_d = dd["eighth"]

    def body():
        idb = p.sb("idb", (128, 128), BF16)
        onesb = p.sb("onesb_s", (1, 128), BF16)
        eighth = p.sb("eighth_s", (1, 128), F32)
        re8 = p.sb("re8", (128, 17), F32)
        abuf = p.sb("abuf", (128, 512), F32)
        mbuf = p.sb("mbuf", (128, 512), F32)
        dbuf = p.sb("dbuf", (128, 512), F32)
        rel_sb = p.sb("relsb", (1, 17), F32)
        bo_sb = p.sb("bosb", (1, D), BF16)
        xb0 = p.sb("xb0", (128, D), F32)
        xb1 = p.sb("xb1", (128, D), F32)
        xbufs = (xb0, xb1)
        xn_bf = p.sb("xnbf", (128, D), BF16)
        sqscr = p.sb("sqscr", (128, D), BF16)

        p.dma("sync", "ldc", idb[:], idb_d)
        p.dma("sync", "ldc", onesb[:], onesb_d)
        p.dma("sync", "ldc", eighth[:], eighth_d)
        p.dma("sync", "ldc", rel_sb[:], rel_d)
        p.dma("gpsimd", "ldo", bo_sb[:], bo_d)
        p.dma("sync", "ldc", mbuf[:], amask_d)

        p.scope()
        pre = p.ps("psre", (128, 32), F32)
        p.wait("tensor", "ldc")
        p.add("tensor", lambda t: t.matmul(pre[:, 0:17], eighth[0:1, :],
                                           rel_sb[0:1, :], start=True, stop=True))
        p.pe_mark()
        p.wait("scalar", "pe")
        p.inc("scalar", "act",
              lambda s: s.activation(re8[:], pre[:, 0:17], AF.Copy))
        p.wait("vector", "act")
        for i in range(17):
            p.dma("sync", "ldd", dbuf[:],
                  diag_d[i:i + 1, :].rearrange("o (q n) -> (o q) n", q=128))
            p.wait("vector", "ldd")
            if i == 0:
                p.inc("vector", "dve", lambda v: v.tensor_tensor(
                    abuf[:], dbuf[:], colap(re8, 17, 0, bcast=512), ALU.mult))
            else:
                p.inc("vector", "dve", lambda v, i=i: v.scalar_tensor_tensor(
                    abuf[:], dbuf[:], colap(re8, 17, i), abuf[:], ALU.mult, ALU.add))
            p.wait("sync", "dve")
        p.end_scope()

        for s_i in range(seqs):
            row0 = s_i * SEQ
            if s_i > 0:
                p.wait("vector", "st")
                p.wait("gpsimd", "st")
                p.wait("gpsimd", "pe")
                p.wait("gpsimd", "dve")
                p.wait("gpsimd", "act")
            p.scope()
            qT = p.sb("qT", (128, NC * SEQ), BF16)
            kT = p.sb("kT", (128, NC * SEQ), BF16)
            vtm = p.sb("vtm", (128, NTS * D), BF16)
            st = ln_stats_tensors(p, f"s{s_i}", NTS)

            p.scope()
            wq_sb = p.sb("wqsb", (128, NC * D), BF16)
            wk_sb = p.sb("wksb", (128, NC * D), BF16)
            wv_sb = p.sb("wvsb", (128, NC * D), BF16)
            for w_sb, w_d in ((wq_sb, wq_d), (wk_sb, wk_d), (wv_sb, wv_d)):
                p.dma("gpsimd", "ldw",
                      w_sb.ap().rearrange("q (c f) -> q c f", c=NC),
                      w_d.rearrange("(c q) f -> q c f", q=128))
            xnT = p.sb("xnT", (128, NC * SEQ), BF16)
            pstr = p.ps("pstr", (128, 1024), BF16)
            emit_ln_stats(p, x_d, st, xbufs, sqscr, NTS, row_base=row0)
            emit_norm_transpose(p, x_d, st, xnT, xbufs, xn_bf, idb, pstr,
                                NTS, row_base=row0)
            p.wait("tensor", "ldw")
            pq = (p.ps("pq0", (128, 512), F32), p.ps("pq1", (128, 512), F32))
            emit_fm_proj(p, wq_sb, D, xnT, qT, pq)
            emit_fm_proj(p, wk_sb, D, xnT, kT, pq)
            _pd = p.at("dve")
            evd = {0: _pd, 1: _pd}
            for ti in range(NTS):
                for nn in range(2):
                    bank = nn
                    pb = pq[bank]
                    if bank in evd:
                        p.wait("tensor", "dve", evd[bank])
                    for c in range(NC):
                        p.add("tensor", lambda t, c=c, ti=ti, nn=nn, pb=pb: t.matmul(
                            pb[:],
                            xnT[:, c * SEQ + ti * 128: c * SEQ + ti * 128 + 128],
                            wv_sb[:, c * D + nn * 512: c * D + nn * 512 + 512],
                            start=(c == 0), stop=(c == NC - 1)))
                    p.pe_mark()
                    p.wait("vector", "pe")
                    p.inc("vector", "dve", lambda v, pb=pb, ti=ti, nn=nn: v.tensor_copy(
                        vtm[:, ti * D + nn * 512: ti * D + nn * 512 + 512], pb[:]))
                    evd[bank] = p.at("dve")
            p.end_scope()

            p.scope()
            oT = p.sb("oT", (128, NC * SEQ), BF16)
            s_sb = p.sb("ssb", (128, SEQ), F32)
            p2 = p.sb("p2", (128, SEQ), BF16)
            p3 = p.sb("p3", (128, SEQ), BF16)
            pT = p.sb("pT", (128, SEQ), BF16)
            mx = p.sb("mx", (128, 8), F32)
            lsum = p.sb("lsum", (128, 2), F32)
            rl = p.sb("rl", (128, 1), F32)
            ps_s = [p.ps(f"pss{i}", (128, 512), F32) for i in range(2)]
            ps_pv = [p.ps(f"pspv{i}", (64, 512), F32) for i in range(2)]
            ps_tr = [p.ps(f"pstr2{i}", (128, 1024), BF16) for i in range(2)]
            p.wait("tensor", "dve")

            for h in range(H):
                par = h % 2
                fc = h // 2
                r0 = 64 * par
                for qi in range(qt):
                    q0 = qi * 128
                    kwin_lo = max(0, q0 - 384)
                    W = q0 + 128
                    ktiles = []
                    kk = 0
                    while kk < kwin_lo:
                        w = min(512, kwin_lo - kk)
                        ktiles.append((kk, w, False))
                        kk += w
                    ktiles.append((kwin_lo, W - kwin_lo, True))
                    sps = ps_s[par]
                    nmax = 0
                    for (kk, w, isdiag) in ktiles:
                        p.add("tensor", lambda t, kk=kk, w=w, sps=sps, r0=r0, fc=fc, q0=q0: t.matmul(
                            sps[:, 0:w], qT[r0:r0 + 64, fc * SEQ + q0: fc * SEQ + q0 + 128],
                            kT[r0:r0 + 64, fc * SEQ + kk: fc * SEQ + kk + w],
                            start=True, stop=True))
                        p.pe_mark()
                        p.wait("vector", "pe")
                        if isdiag:
                            aoff = 512 - w
                            p.add("vector", lambda v, kk=kk, w=w, sps=sps, aoff=aoff: v.tensor_mul(
                                s_sb[:, kk:kk + w], sps[:, 0:w], abuf[:, aoff:aoff + w]))
                            p.add("vector", lambda v, kk=kk, w=w, aoff=aoff: v.tensor_add(
                                s_sb[:, kk:kk + w], s_sb[:, kk:kk + w], mbuf[:, aoff:aoff + w]))
                        else:
                            p.add("vector", lambda v, kk=kk, w=w, sps=sps: v.tensor_single_scalar(
                                s_sb[:, kk:kk + w], sps[:, 0:w], colap(re8, 17, 16), ALU.mult))
                        p.inc("vector", "dve", lambda v, kk=kk, w=w, nmax=nmax: v.reduce_max(
                            mx[:, nmax:nmax + 1], s_sb[:, kk:kk + w], AX.X))
                        nmax += 1
                        p.wait("tensor", "dve")
                    for i in range(1, nmax):
                        p.add("vector", lambda v, i=i: v.tensor_max(
                            mx[:, 0:1], mx[:, 0:1], mx[:, i:i + 1]))
                    p.inc("vector", "dve",
                          lambda v: v.tensor_scalar_mul(mx[:, 1:2], mx[:, 0:1], -1.0))
                    p.wait("scalar", "dve")
                    p.inc("scalar", "act", lambda s, W=W: s.activation(
                        p2[:, 0:W], s_sb[:, 0:W], AF.Exp, bias=colap(mx, 8, 1),
                        accum_out=lsum[:, 0:1]))
                    p.wait("vector", "act")
                    p.add("vector", lambda v: v.reciprocal(rl[:], lsum[:, 0:1]))
                    p.inc("vector", "dve", lambda v, W=W: v.tensor_scalar_mul(
                        p3[:, 0:W], p2[:, 0:W], colap(rl, 1, 0)))
                    nkb = W // 128
                    p.wait("tensor", "dve")
                    spt = ps_tr[par]
                    for g0 in range(0, nkb, 8):
                        gn = min(8, nkb - g0)
                        for gi in range(gn):
                            p.add("tensor", lambda t, g0=g0, gi=gi, spt=spt: t.transpose(
                                spt[:, gi * 128:(gi + 1) * 128],
                                p3[:, (g0 + gi) * 128:(g0 + gi + 1) * 128], idb[:]))
                        p.pe_mark()
                        p.wait("vector", "pe")
                        p.inc("vector", "dve", lambda v, g0=g0, gn=gn, spt=spt: v.tensor_copy(
                            pT[:, g0 * 128:(g0 + gn) * 128], spt[:, 0:gn * 128]))
                        p.wait("tensor", "dve")
                    spv = ps_pv[par]
                    for kb in range(nkb):
                        p.add("tensor", lambda t, kb=kb, h=h, spv=spv, nkb=nkb: t.matmul(
                            spv[0:64, 0:128],
                            vtm[:, kb * D + h * HD: kb * D + h * HD + HD],
                            pT[:, kb * 128:(kb + 1) * 128],
                            start=(kb == 0), stop=(kb == nkb - 1)))
                    p.pe_mark()
                    p.wait("vector", "pe")
                    p.inc("vector", "dve", lambda v, spv=spv, fc=fc, q0=q0, r0=r0: v.tensor_copy(
                        oT[r0:r0 + 64, fc * SEQ + q0: fc * SEQ + q0 + 128],
                        spv[0:64, 0:128]))
                    p.wait("tensor", "dve")

            p.scope()
            wo_sb = p.sb("wosb", (128, NC * D), BF16)
            p.wait("gpsimd", "pe")
            p.wait("gpsimd", "dve")
            p.wait("gpsimd", "act")
            p.dma("gpsimd", "ldo",
                  wo_sb.ap().rearrange("q (c f) -> q c f", c=NC),
                  wo_d.rearrange("(c q) f -> q c f", q=128))
            po = (p.ps("po0", (128, 512), F32), p.ps("po1", (128, 512), F32))
            xout = p.sb("xout", (128, D), F32)
            p.wait("tensor", "ldo")
            p.wait("sync", "act")
            p.wait("sync", "dve")
            evd2 = {}
            st_hist = 0
            xwar = {}
            for ti in range(qt):
                if ti >= 2:
                    p.wait("sync", "dve", xwar[ti - 2])
                else:
                    p.wait("sync", "dve")
                    p.wait("sync", "act")
                p.dma("sync", f"ldx{ti % 2}", xbufs[ti % 2][:],
                      x_d[row0 + ti * 128: row0 + ti * 128 + 128, :])
                for nn in range(2):
                    pb = po[nn]
                    if nn in evd2:
                        p.wait("tensor", "dve", evd2[nn])
                    p.add("tensor", lambda t, pb=pb, nn=nn: t.matmul(
                        pb[:], onesb[0:1, :], bo_sb[0:1, nn * 512: nn * 512 + 512],
                        start=True, stop=False))
                    for c in range(NC):
                        p.add("tensor", lambda t, c=c, ti=ti, nn=nn, pb=pb: t.matmul(
                            pb[:], oT[:, c * SEQ + ti * 128: c * SEQ + ti * 128 + 128],
                            wo_sb[:, c * D + nn * 512: c * D + nn * 512 + 512],
                            start=False, stop=(c == NC - 1)))
                    p.pe_mark()
                    p.wait("vector", "pe")
                    p.wait("vector", f"ldx{ti % 2}")
                    if ti >= 1:
                        p.wait("vector", "st", st_hist)
                    p.inc("vector", "dve", lambda v, pb=pb, ti=ti, nn=nn: v.tensor_add(
                        xout[:, nn * 512: nn * 512 + 512], pb[:],
                        xbufs[ti % 2][:, nn * 512: nn * 512 + 512]))
                    evd2[nn] = p.at("dve")
                xwar[ti] = p.at("dve")
                p.wait("gpsimd", "dve")
                p.dma("gpsimd", "st",
                      xo_d[row0 + ti * 128: row0 + ti * 128 + 128, :], xout[:])
                st_hist = p.at("st")
            p.end_scope()
            p.end_scope()
            p.end_scope()

        p.wait("gpsimd", "st")

    if mega is None:
        with es, nc.Block() as block:
            p.scope(); body(); p.end_scope()
            p.finalize(block)
        return nc
    hard_barrier(p)
    p.scope(); body(); p.end_scope()
    return None


def _shard_x(x):
    x = np.asarray(x, np.float32)
    n = x.reshape(-1, SEQ, D).shape[0] // BPC
    x = np.ascontiguousarray(x.reshape(n * BPC, SEQ, D))
    return [np.ascontiguousarray(x[i * BPC:(i + 1) * BPC].reshape(TPC, D))
            for i in range(n)]


_ATTN_A_CACHE = {}


def run_attn_a(x_full, inputs, cores=None, seqs=BPC, qt=NTS):
    key = (seqs, qt)
    if key not in _ATTN_A_CACHE:
        _ATTN_A_CACHE[key] = build_attn_a(seqs, qt)
    nc = _ATTN_A_CACHE[key]
    shards = _shard_x(x_full)
    base = dict(
        wq=np.ascontiguousarray(inputs["L0_attn_a_Wq"], np.float32),
        wk=np.ascontiguousarray(inputs["L0_attn_a_Wk"], np.float32),
        wv=np.ascontiguousarray(inputs["L0_attn_a_Wv"], np.float32),
        wo=np.ascontiguousarray(inputs["L0_attn_a_Wo"], np.float32),
        bo=np.ascontiguousarray(inputs["L0_attn_a_bo"], np.float32).reshape(1, D),
        rel=np.ascontiguousarray(inputs["L0_attn_a_rel_emb"], np.float32).reshape(1, 17),
        adiag=_CONSTS["AA_DIAG"],
        amask=_CONSTS["AA_MASK"],
        identb=_CONSTS["IDENT_BF"],
        onesb=_CONSTS["ONES_BF"],
        eighth=_CONSTS["EIGHTH_F32"],
    )
    cores = cores or list(range(NCORES))
    in_maps = [dict(base, x=s) for s in shards[:len(cores)]]
    res = run_bass_kernel_spmd(nc, in_maps, core_ids=cores)
    out = np.stack([np.asarray(r["xo"]) for r in res.results])
    return out


def _ffn_both(x, inputs):
    x3 = x.reshape(B, SEQ, D)
    outs = np.empty_like(x3)
    for half in range(BPC):
        sel = np.ascontiguousarray(x3[half::BPC])
        o = run_ffn(sel, inputs)
        outs[half::BPC] = o.reshape(NCORES, SEQ, D)
    return outs


def kernel_unfused(**inputs):
    x = np.asarray(inputs["L0_attn_a_x"], np.float32)
    x = run_attn_a(x, inputs).reshape(B, SEQ, D)
    x = _ffn_both(x, inputs)
    x = run_attn_bl1("b", x, inputs).reshape(B, SEQ, D)
    x = _ffn_both(x, inputs)
    x = run_l1a(x, inputs).reshape(B, SEQ, D)
    x = run_moe(x, inputs).reshape(B, SEQ, D)
    x = run_attn_bl1("l1b", x, inputs).reshape(B, SEQ, D)
    x = run_moe(x, inputs).reshape(B, SEQ, D)
    return x.astype(np.float32)


def emit_weight_transpose(p, w_d, w_sb, idb, dout, din, tag):
    noc = dout // 128
    nic = din // 128
    p.scope()
    wtmp = p.sb(f"wtp{tag}", (128, din), BF16)
    pst = p.ps(f"pwt{tag}", (128, 1024), BF16)
    for oc in range(noc):
        p.wait("gpsimd", "pe")
        p.dma("gpsimd", "ldw", wtmp[:], w_d[oc * 128:(oc + 1) * 128, :])
        p.wait("tensor", "ldw")
        p.wait("tensor", "ldc")
        for g0 in range(0, nic, 8):
            gn = min(8, nic - g0)
            p.wait("tensor", "dve")
            for gi in range(gn):
                ic = g0 + gi
                p.add("tensor", lambda t, ic=ic, gi=gi: t.transpose(
                    pst[:, gi * 128:(gi + 1) * 128],
                    wtmp[:, ic * 128:(ic + 1) * 128], idb[:]))
            p.pe_mark()
            p.wait("vector", "pe")
            for gi in range(gn):
                ic = g0 + gi
                p.inc("vector", "dve", lambda v, ic=ic, gi=gi, oc=oc: v.tensor_copy(
                    w_sb[:, ic * dout + oc * 128: ic * dout + oc * 128 + 128],
                    pst[:, gi * 128:(gi + 1) * 128]))
    p.end_scope()


def _dram_attn_bl1(nc, kind, pref=""):
    mk = _mk_dram(nc, pref)
    d = dict(wp=mk("wp", [D, D]), bp=mk("bp", [1, D]),
             amask=mk("amask", [128, 512]), identb=mk("identb", [128, 128], BF16),
             onesb=mk("onesb", [1, 128], BF16), sccol=mk("sccol", [128, 1]))
    if kind == "b":
        d.update(wq=mk("wq", [D, D]), wk=mk("wk", [D, D]), wv=mk("wv", [D, D]),
                 bq=mk("bq", [1, D]), bk=mk("bk", [1, D]), bv=mk("bv", [1, D]),
                 ropec=mk("ropec", [SEQ, D], BF16), ropes=mk("ropes", [SEQ, D], BF16))
    else:
        d.update(wq=mk("wq", [HD, HD]), wk=mk("wk", [HD, HD]), wv=mk("wv", [HD, HD]))
    return d


def build_attn_bl1(kind, seqs=BPC, qt=NTS, mega=None):
    assert kind in ("b", "l1b")
    if mega is None:
        nc = bass.Bass("TRN2", target_bir_lowering=False)
        es = ExitStack()
        p = Prog(nc, es)
        dd = _dram_attn_bl1(nc, kind)
        x_d = nc.dram_tensor("x", [TPC, D], F32, kind="ExternalInput").ap()
        xo_d = nc.dram_tensor("xo", [TPC, D], F32, kind="ExternalOutput").ap()
    else:
        p, dd, x_d, xo_d = mega
        nc = p.nc
    wq_d = dd["wq"]; wk_d = dd["wk"]; wv_d = dd["wv"]
    wp_d = dd["wp"]; bp_d = dd["bp"]; amask_d = dd["amask"]
    idb_d = dd["identb"]; onesb_d = dd["onesb"]; sccol_d = dd["sccol"]
    if kind == "b":
        bq_d = dd["bq"]; bk_d = dd["bk"]; bv_d = dd["bv"]
        cos_d = dd["ropec"]; sin_d = dd["ropes"]

    def body():
        idb = p.sb("idb", (128, 128), BF16)
        onesb = p.sb("onesb_s", (1, 128), BF16)
        mbuf = p.sb("mbuf", (128, 512), F32)
        sccol = p.sb("sccol_s", (128, 1), F32)
        xb0 = p.sb("xb0", (128, D), F32)
        xb1 = p.sb("xb1", (128, D), F32)
        xbufs = (xb0, xb1)
        xn_bf = p.sb("xnbf", (128, D), BF16)
        sqscr = xn_bf
        bp_sb = p.sb("bpsb", (1, D), BF16)

        p.dma("sync", "ldc", idb[:], idb_d)
        p.dma("sync", "ldc", onesb[:], onesb_d)
        p.dma("sync", "ldc", mbuf[:], amask_d)
        p.dma("sync", "ldc", sccol[:], sccol_d)
        p.dma("gpsimd", "ldw", bp_sb[:], bp_d)

        if kind == "b":
            bq_sb = p.sb("bqsb", (1, D), BF16)
            bk_sb = p.sb("bksb", (1, D), BF16)
            bv_sb = p.sb("bvsb", (1, D), BF16)
            p.dma("gpsimd", "ldw", bq_sb[:], bq_d)
            p.dma("gpsimd", "ldw", bk_sb[:], bk_d)
            p.dma("gpsimd", "ldw", bv_sb[:], bv_d)
        else:
            bdq = p.sb("bdq", (128, 128), BF16)
            bdk = p.sb("bdk", (128, 128), BF16)
            bdv = p.sb("bdv", (128, 128), BF16)
            p.scope()
            wtmp = p.sb("wtmp64", (64, HD), BF16)
            pst = p.ps("pwt64", (128, 1024), BF16)
            for bd, w_d2 in ((bdq, wq_d), (bdk, wk_d), (bdv, wv_d)):
                p.add("gpsimd", lambda g, bd=bd: g.memset(bd[:], 0.0))
                p.wait("gpsimd", "pe")
                p.dma("gpsimd", "ldw", wtmp[:], w_d2)
                p.wait("tensor", "ldw")
                p.wait("tensor", "ldc")
                p.wait("tensor", "dve")
                p.wait("tensor", "pool")
                p.add("tensor", lambda t: t.transpose(
                    pst[0:64, 0:64], wtmp[0:64, 0:64], idb[0:64, 0:64]))
                p.pe_mark()
                p.wait("vector", "pe")
                p.wait("vector", "pool")
                p.inc("vector", "dve", lambda v, bd=bd: v.tensor_copy(
                    bd[0:64, 0:64], pst[0:64, 0:64]))
                p.inc("vector", "dve", lambda v, bd=bd: v.tensor_copy(
                    bd[64:128, 64:128], pst[0:64, 0:64]))
            p.end_scope()

        for s_i in range(seqs):
            row0 = s_i * SEQ
            if s_i > 0:
                p.wait("vector", "st")
                p.wait("gpsimd", "st")
                p.wait("gpsimd", "pe")
                p.wait("gpsimd", "dve")
                p.wait("gpsimd", "act")
            p.scope()
            qT = p.sb("qT", (128, NC * SEQ), BF16)
            kT = p.sb("kT", (128, NC * SEQ), BF16)
            vtm = p.sb("vtm", (128, NTS * D), BF16)
            st = ln_stats_tensors(p, f"s{s_i}", NTS)

            p.scope()
            xnT = p.sb("xnT", (128, NC * SEQ), BF16)
            pstr = p.ps("pstr", (128, 1024), BF16)
            emit_ln_stats(p, x_d, st, xbufs, sqscr, NTS, row_base=row0)
            emit_norm_transpose(p, x_d, st, xnT, xbufs, xn_bf, idb, pstr,
                                NTS, row_base=row0)

            if kind == "b":
                wqT = p.sb("wqT", (128, NC * D), BF16)
                wkT = p.sb("wkT", (128, NC * D), BF16)
                wvT = p.sb("wvT", (128, NC * D), BF16)
                p.wait("gpsimd", "pe")
                p.wait("gpsimd", "dve")
                p.wait("gpsimd", "act")
                emit_weight_transpose(p, wq_d, wqT, idb, D, D, f"q{s_i}")
                emit_weight_transpose(p, wk_d, wkT, idb, D, D, f"k{s_i}")
                emit_weight_transpose(p, wv_d, wvT, idb, D, D, f"v{s_i}")
                p.wait("sync", "pe")
                p.wait("sync", "dve")
                cosb = p.sb("cosb", (128, D), BF16)
                sinb = p.sb("sinb", (128, D), BF16)
                qtm = p.sb("qtm", (128, D), BF16)
                ktm = qtm
                r1 = p.sb("ropet1", (128, D), BF16)
                r2 = p.sb("ropet2", (128, D), BF16)
                psq = p.ps("psq", (128, 1024), F32)
                psk = p.ps("psk", (128, 1024), F32)
                psv = p.ps("psv", (128, 1024), F32)
                pstr2 = p.ps("pstr2b", (128, 1024), BF16)
                for ti in range(NTS):
                    p.wait("sync", "dve")
                    p.dma("sync", f"ldr{ti % 2}", cosb[:],
                          cos_d[ti * 128:(ti + 1) * 128, :])
                    p.dma("sync", f"ldr{ti % 2}", sinb[:],
                          sin_d[ti * 128:(ti + 1) * 128, :])
                    p.wait("tensor", "dve")
                    for (ps_, wT, bb) in ((psq, wqT, bq_sb), (psk, wkT, bk_sb),
                                          (psv, wvT, bv_sb)):
                        for nn in range(2):
                            p.add("tensor", lambda t, ps_=ps_, bb=bb, nn=nn: t.matmul(
                                ps_[:, nn * 512:(nn + 1) * 512], onesb[0:1, :],
                                bb[0:1, nn * 512:(nn + 1) * 512],
                                start=True, stop=False))
                            for c in range(NC):
                                p.add("tensor", lambda t, ps_=ps_, wT=wT, c=c, ti=ti, nn=nn: t.matmul(
                                    ps_[:, nn * 512:(nn + 1) * 512],
                                    xnT[:, c * SEQ + ti * 128: c * SEQ + ti * 128 + 128],
                                    wT[:, c * D + nn * 512: c * D + nn * 512 + 512],
                                    start=False, stop=(c == NC - 1)))
                    p.pe_mark()
                    p.wait("vector", "pe")
                    p.wait("vector", f"ldr{ti % 2}")
                    swap = lambda ps_: bass.AP(ps_, 1, [[1024, 128], [2, 512], [-1, 2]])
                    p.inc("vector", "dve", lambda v, ti=ti: v.tensor_copy(
                        vtm[:, ti * D: (ti + 1) * D], psv[:, :]))
                    for (ps_, dstT) in ((psq, qT), (psk, kT)):
                        p.add("vector", lambda v, ps_=ps_: v.tensor_mul(r1[:], ps_[:, :], cosb[:]))
                        p.add("vector", lambda v, ps_=ps_: v.tensor_mul(r2[:], swap(ps_), sinb[:]))
                        p.inc("vector", "dve", lambda v: v.tensor_add(qtm[:], r1[:], r2[:]))
                        p.wait("tensor", "dve")
                        for cc in range(NC):
                            p.add("tensor", lambda t, cc=cc: t.transpose(
                                pstr2[:, cc * 128:(cc + 1) * 128],
                                qtm[:, cc * 128:(cc + 1) * 128], idb[:]))
                        p.pe_mark()
                        p.wait("vector", "pe")
                        dst3 = dstT.ap().rearrange("q (c t) -> q c t", c=NC)
                        p.inc("vector", "dve", lambda v, dst3=dst3, ti=ti: v.tensor_copy(
                            dst3[:, :, ti * 128:(ti + 1) * 128], pstr2[:, :]))
                        p.wait("vector", "pe")
            else:
                psq = p.ps("psq", (128, 512), F32)
                psk = p.ps("psk", (128, 512), F32)
                psv = p.ps("psv", (128, 1024), F32)
                evq = 0
                for c in range(NC):
                    for tg in range(SEQ // 512):
                        p.wait("tensor", "dve", evq)
                        p.add("tensor", lambda t, c=c, tg=tg: t.matmul(
                            psq[:, :], bdq[:, :],
                            xnT[:, c * SEQ + tg * 512: c * SEQ + tg * 512 + 512],
                            start=True, stop=True))
                        p.add("tensor", lambda t, c=c, tg=tg: t.matmul(
                            psk[:, :], bdk[:, :],
                            xnT[:, c * SEQ + tg * 512: c * SEQ + tg * 512 + 512],
                            start=True, stop=True))
                        p.pe_mark()
                        p.wait("vector", "pe")
                        p.inc("vector", "dve", lambda v, c=c, tg=tg: v.tensor_copy(
                            qT[:, c * SEQ + tg * 512: c * SEQ + tg * 512 + 512], psq[:, :]))
                        p.inc("vector", "dve", lambda v, c=c, tg=tg: v.tensor_copy(
                            kT[:, c * SEQ + tg * 512: c * SEQ + tg * 512 + 512], psk[:, :]))
                        evq = p.at("dve")
                for ti in range(NTS):
                    p.wait("tensor", "dve")
                    for c in range(NC):
                        p.add("tensor", lambda t, c=c, ti=ti: t.matmul(
                            psv[:, c * 128:(c + 1) * 128] if c < 4 else psv[:, c * 128:(c + 1) * 128],
                            xnT[:, c * SEQ + ti * 128: c * SEQ + ti * 128 + 128],
                            bdv[:, :], start=True, stop=True))
                    p.pe_mark()
                    p.wait("vector", "pe")
                    p.inc("vector", "dve", lambda v, ti=ti: v.tensor_copy(
                        vtm[:, ti * D: (ti + 1) * D], psv[:, :]))
            p.end_scope()

            p.scope()
            oT = p.sb("oT", (128, NC * SEQ), BF16)
            s_sb = p.sb("ssb", (128, SEQ), F32)
            p2 = p.sb("p2", (128, SEQ), BF16)
            p3 = p.sb("p3", (128, SEQ), BF16)
            pT = p.sb("pT", (128, SEQ), BF16)
            mx = p.sb("mx", (128, 8), F32)
            lsum = p.sb("lsum", (128, 2), F32)
            rl = p.sb("rl", (128, 1), F32)
            ps_s = [p.ps(f"pss{i}", (128, 512), F32) for i in range(2)]
            ps_pv = [p.ps(f"pspv{i}", (64, 512), F32) for i in range(2)]
            ps_tr = [p.ps(f"pstr3{i}", (128, 1024), BF16) for i in range(2)]
            p.wait("tensor", "dve")

            for h in range(H):
                par = h % 2
                fc = h // 2
                r0 = 64 * par
                for qi in range(qt):
                    q0 = qi * 128
                    kwin_lo = max(0, q0 - 384)
                    W = q0 + 128
                    ktiles = []
                    kk = 0
                    while kk < kwin_lo:
                        w = min(512, kwin_lo - kk)
                        ktiles.append((kk, w, False))
                        kk += w
                    ktiles.append((kwin_lo, W - kwin_lo, True))
                    sps = ps_s[par]
                    nmax = 0
                    for (kk, w, isdiag) in ktiles:
                        p.add("tensor", lambda t, kk=kk, w=w, sps=sps, r0=r0, fc=fc, q0=q0: t.matmul(
                            sps[:, 0:w], qT[r0:r0 + 64, fc * SEQ + q0: fc * SEQ + q0 + 128],
                            kT[r0:r0 + 64, fc * SEQ + kk: fc * SEQ + kk + w],
                            start=True, stop=True))
                        p.pe_mark()
                        p.wait("vector", "pe")
                        if isdiag:
                            aoff = 512 - w
                            p.add("vector", lambda v, kk=kk, w=w, sps=sps, aoff=aoff: v.tensor_add(
                                s_sb[:, kk:kk + w], sps[:, 0:w], mbuf[:, aoff:aoff + w]))
                        else:
                            p.add("vector", lambda v, kk=kk, w=w, sps=sps: v.tensor_copy(
                                s_sb[:, kk:kk + w], sps[:, 0:w]))
                        p.inc("vector", "dve", lambda v, kk=kk, w=w, nmax=nmax: v.reduce_max(
                            mx[:, nmax:nmax + 1], s_sb[:, kk:kk + w], AX.X))
                        nmax += 1
                        p.wait("tensor", "dve")
                    for i in range(1, nmax):
                        p.add("vector", lambda v, i=i: v.tensor_max(
                            mx[:, 0:1], mx[:, 0:1], mx[:, i:i + 1]))
                    p.inc("vector", "dve", lambda v: v.tensor_mul(
                        mx[:, 1:2], mx[:, 0:1], sccol[:, 0:1]))
                    p.wait("scalar", "dve")
                    p.inc("scalar", "act", lambda s, W=W: s.activation(
                        p2[:, 0:W], s_sb[:, 0:W], AF.Exp, bias=colap(mx, 8, 1),
                        scale=sccol[:, 0:1], accum_out=lsum[:, 0:1]))
                    p.wait("vector", "act")
                    p.add("vector", lambda v: v.reciprocal(rl[:], lsum[:, 0:1]))
                    p.inc("vector", "dve", lambda v, W=W: v.tensor_scalar_mul(
                        p3[:, 0:W], p2[:, 0:W], colap(rl, 1, 0)))
                    nkb = W // 128
                    p.wait("tensor", "dve")
                    spt = ps_tr[par]
                    for g0 in range(0, nkb, 8):
                        gn = min(8, nkb - g0)
                        for gi in range(gn):
                            p.add("tensor", lambda t, g0=g0, gi=gi, spt=spt: t.transpose(
                                spt[:, gi * 128:(gi + 1) * 128],
                                p3[:, (g0 + gi) * 128:(g0 + gi + 1) * 128], idb[:]))
                        p.pe_mark()
                        p.wait("vector", "pe")
                        p.inc("vector", "dve", lambda v, g0=g0, gn=gn, spt=spt: v.tensor_copy(
                            pT[:, g0 * 128:(g0 + gn) * 128], spt[:, 0:gn * 128]))
                        p.wait("tensor", "dve")
                    spv = ps_pv[par]
                    for kb in range(nkb):
                        p.add("tensor", lambda t, kb=kb, h=h, spv=spv, nkb=nkb: t.matmul(
                            spv[0:64, 0:128],
                            vtm[:, kb * D + h * HD: kb * D + h * HD + HD],
                            pT[:, kb * 128:(kb + 1) * 128],
                            start=(kb == 0), stop=(kb == nkb - 1)))
                    p.pe_mark()
                    p.wait("vector", "pe")
                    p.inc("vector", "dve", lambda v, spv=spv, fc=fc, q0=q0, r0=r0: v.tensor_copy(
                        oT[r0:r0 + 64, fc * SEQ + q0: fc * SEQ + q0 + 128],
                        spv[0:64, 0:128]))
                    p.wait("tensor", "dve")

            p.scope()
            wpT = p.sb("wpT", (128, NC * D), BF16)
            p.wait("gpsimd", "pe")
            p.wait("gpsimd", "dve")
            p.wait("gpsimd", "act")
            emit_weight_transpose(p, wp_d, wpT, idb, D, D, f"p{s_i}")
            p.wait("sync", "pe")
            p.wait("tensor", "dve")
            po = (p.ps("po0", (128, 512), F32), p.ps("po1", (128, 512), F32))
            xout = p.sb("xout", (128, D), F32)
            p.wait("tensor", "ldw")
            p.wait("sync", "act")
            p.wait("sync", "dve")
            evd2 = {}
            st_hist = 0
            xwar = {}
            for ti in range(qt):
                if ti >= 2:
                    p.wait("sync", "dve", xwar[ti - 2])
                else:
                    p.wait("sync", "dve")
                    p.wait("sync", "act")
                p.dma("sync", f"ldx{ti % 2}", xbufs[ti % 2][:],
                      x_d[row0 + ti * 128: row0 + ti * 128 + 128, :])
                for nn in range(2):
                    pb = po[nn]
                    if evd2.get(nn):
                        p.wait("tensor", "dve", evd2[nn])
                    p.add("tensor", lambda t, pb=pb, nn=nn: t.matmul(
                        pb[:], onesb[0:1, :], bp_sb[0:1, nn * 512: nn * 512 + 512],
                        start=True, stop=False))
                    for c in range(NC):
                        p.add("tensor", lambda t, c=c, ti=ti, nn=nn, pb=pb: t.matmul(
                            pb[:], oT[:, c * SEQ + ti * 128: c * SEQ + ti * 128 + 128],
                            wpT[:, c * D + nn * 512: c * D + nn * 512 + 512],
                            start=False, stop=(c == NC - 1)))
                    p.pe_mark()
                    p.wait("vector", "pe")
                    p.wait("vector", f"ldx{ti % 2}")
                    if ti >= 1:
                        p.wait("vector", "st", st_hist)
                    p.inc("vector", "dve", lambda v, pb=pb, ti=ti, nn=nn: v.tensor_add(
                        xout[:, nn * 512: nn * 512 + 512], pb[:],
                        xbufs[ti % 2][:, nn * 512: nn * 512 + 512]))
                    evd2[nn] = p.at("dve")
                xwar[ti] = p.at("dve")
                p.wait("gpsimd", "dve")
                p.dma("gpsimd", "st",
                      xo_d[row0 + ti * 128: row0 + ti * 128 + 128, :], xout[:])
                st_hist = p.at("st")
            p.end_scope()
            p.end_scope()
            p.end_scope()

        p.wait("gpsimd", "st")

    if mega is None:
        with es, nc.Block() as block:
            p.scope(); body(); p.end_scope()
            p.finalize(block)
        return nc
    hard_barrier(p)
    p.scope(); body(); p.end_scope()
    return None


_BL1_CACHE = {}


def run_attn_bl1(kind, x_full, inputs, cores=None, seqs=BPC, qt=NTS):
    key = (kind, seqs, qt)
    if key not in _BL1_CACHE:
        _BL1_CACHE[key] = build_attn_bl1(kind, seqs, qt)
    nc = _BL1_CACHE[key]
    shards = _shard_x(x_full)
    if kind == "b":
        base = dict(
            wq=np.ascontiguousarray(inputs["L0_attn_b_wq"], np.float32),
            wk=np.ascontiguousarray(inputs["L0_attn_b_wk"], np.float32),
            wv=np.ascontiguousarray(inputs["L0_attn_b_wv"], np.float32),
            wp=np.ascontiguousarray(inputs["L0_attn_b_wp"], np.float32),
            bq=np.ascontiguousarray(inputs["L0_attn_b_bq"], np.float32).reshape(1, D),
            bk=np.ascontiguousarray(inputs["L0_attn_b_bk"], np.float32).reshape(1, D),
            bv=np.ascontiguousarray(inputs["L0_attn_b_bv"], np.float32).reshape(1, D),
            bp=np.ascontiguousarray(inputs["L0_attn_b_bp"], np.float32).reshape(1, D),
            ropec=_CONSTS["ROPE_COS"], ropes=_CONSTS["ROPE_SIN"],
            sccol=np.full((128, 1), 0.125, np.float32),
        )
    else:
        base = dict(
            wq=np.ascontiguousarray(inputs["L1_attn_b_Wq"], np.float32),
            wk=np.ascontiguousarray(inputs["L1_attn_b_Wk"], np.float32),
            wv=np.ascontiguousarray(inputs["L1_attn_b_Wv"], np.float32),
            wp=np.ascontiguousarray(inputs["L1_attn_b_Wu"], np.float32),
            bp=np.ascontiguousarray(inputs["L1_attn_b_bu"], np.float32).reshape(1, D),
            sccol=np.full((128, 1), float(D) ** -0.5, np.float32),
        )
    base.update(amask=_CONSTS["AA_MASK"], identb=_CONSTS["IDENT_BF"],
                onesb=_CONSTS["ONES_BF"])
    cores = cores or list(range(NCORES))
    in_maps = [dict(base, x=s) for s in shards[:len(cores)]]
    res = run_bass_kernel_spmd(nc, in_maps, core_ids=cores)
    return np.stack([np.asarray(r["xo"]) for r in res.results])


def _dram_moe(nc, pref=""):
    mk = _mk_dram(nc, pref)
    NE = 8
    return dict(wex=mk("wex", [NE * D, D]), bex=mk("bex", [NE, D]),
                wrt=mk("wrt", [NE, D]), brt=mk("brt", [1, NE]),
                identb=mk("identb", [128, 128], BF16),
                identf=mk("identf", [128, 128]),
                onesb=mk("onesb", [1, 128], BF16))


def build_moe(seqs=BPC, qt=NTS, mega=None):
    NE = 8
    if mega is None:
        nc = bass.Bass("TRN2", target_bir_lowering=False)
        es = ExitStack()
        p = Prog(nc, es)
        dd = _dram_moe(nc)
        x_d = nc.dram_tensor("x", [TPC, D], F32, kind="ExternalInput").ap()
        xo_d = nc.dram_tensor("xo", [TPC, D], F32, kind="ExternalOutput").ap()
    else:
        p, dd, x_d, xo_d = mega
        nc = p.nc
    we_d = dd["wex"]; be_d = dd["bex"]; wr_d = dd["wrt"]; br_d = dd["brt"]
    idb_d = dd["identb"]; idf_d = dd["identf"]; onesb_d = dd["onesb"]

    def body():
        idb = p.sb("idb", (128, 128), BF16)
        idf = p.sb("idf", (128, 128), F32)
        onesb = p.sb("onesb_s", (1, 128), BF16)
        xb0 = p.sb("xb0", (128, D), F32)
        xb1 = p.sb("xb1", (128, D), F32)
        xbufs = (xb0, xb1)
        xn_bf = p.sb("xnbf", (128, D), BF16)
        be_sb = p.sb("besb", (NE, D), BF16)
        br_sb = p.sb("brsb", (1, NE), BF16)
        wrT = p.sb("wrT", (128, NC * NE), BF16)
        wr_tmp = p.sb("wrtmp", (NE, D), BF16)

        p.dma("sync", "ldc", idb[:], idb_d)
        p.dma("sync", "ldc", idf[:], idf_d)
        p.dma("sync", "ldc", onesb[:], onesb_d)
        p.dma("gpsimd", "ldw", be_sb[:], be_d)
        p.dma("gpsimd", "ldw", br_sb[:], br_d)
        p.dma("gpsimd", "ldw", wr_tmp[:], wr_d)

        p.scope()
        pwr = p.ps("pwr", (128, 64), BF16)
        p.wait("tensor", "ldc")
        p.wait("tensor", "ldw")
        for c in range(NC):
            p.add("tensor", lambda t, c=c: t.transpose(
                pwr[:, c * 8:(c + 1) * 8], wr_tmp[0:8, c * 128:(c + 1) * 128],
                idb[0:8, 0:8]))
        p.pe_mark()
        p.wait("vector", "pe")
        p.inc("vector", "dve", lambda v: v.tensor_copy(wrT[:, 0:NC * NE], pwr[:, 0:NC * NE]))
        p.end_scope()

        for s_i in range(seqs):
            row0 = s_i * SEQ
            if s_i > 0:
                p.wait("vector", "st")
                p.wait("gpsimd", "st")
                p.wait("gpsimd", "pe")
                p.wait("gpsimd", "dve")
                p.wait("gpsimd", "act")
            p.scope()
            st = ln_stats_tensors(p, f"s{s_i}", NTS)
            xnT = p.sb("xnT", (128, NC * SEQ), BF16)
            wk_all = p.sb("wkept", (128, NTS * NE), F32)
            wTb = p.sb("wTb", (8, NTS * 128), BF16)
            pstr = p.ps("pstr", (128, 1024), BF16)
            emit_ln_stats(p, x_d, st, xbufs, xn_bf, NTS, row_base=row0)
            emit_norm_transpose(p, x_d, st, xnT, xbufs, xn_bf, idb, pstr,
                                NTS, row_base=row0)

            p.scope()
            rbuf = p.sb("rbuf", (128, NE), F32)
            m1 = p.sb("m1", (128, 4), F32)
            lsum = p.sb("lsumr", (128, 1), F32)
            rl = p.sb("rlr", (128, 1), F32)
            eq = p.sb("eqr", (128, NE), F32)
            pr = p.ps("pr", (128, 64), F32)
            pwt = p.ps("pwt2", (128, 128), F32)
            for ti in range(qt):
                p.wait("tensor", "dve")
                p.add("tensor", lambda t: t.matmul(
                    pr[:, 0:NE], onesb[0:1, :], br_sb[0:1, :], start=True, stop=False))
                for c in range(NC):
                    p.add("tensor", lambda t, c=c, ti=ti: t.matmul(
                        pr[:, 0:NE], xnT[:, c * SEQ + ti * 128: c * SEQ + ti * 128 + 128],
                        wrT[:, c * NE:(c + 1) * NE], start=False, stop=(c == NC - 1)))
                p.pe_mark()
                p.wait("vector", "pe")
                p.inc("vector", "dve", lambda v: v.reduce_max(m1[:, 0:1], pr[:, 0:NE], AX.X))
                p.add("vector", lambda v: v.tensor_scalar_mul(m1[:, 1:2], m1[:, 0:1], -1.0))
                p.wait("scalar", "dve")
                p.inc("scalar", "act", lambda s: s.activation(
                    rbuf[:], pr[:, 0:NE], AF.Exp, bias=colap(m1, 4, 1),
                    accum_out=lsum[:, 0:1]))
                p.wait("vector", "act")
                p.add("vector", lambda v: v.reciprocal(rl[:], lsum[:, 0:1]))
                p.add("vector", lambda v: v.tensor_scalar_mul(rbuf[:], rbuf[:], colap(rl, 1, 0)))
                p.add("vector", lambda v: v.reduce_max(m1[:, 0:1], rbuf[:], AX.X))
                p.add("vector", lambda v: v.tensor_scalar(
                    eq[:], rbuf[:], colap(m1, 4, 0), None, ALU.is_ge))
                p.add("vector", lambda v: v.scalar_tensor_tensor(
                    eq[:], eq[:], -2.0, rbuf[:], ALU.mult, ALU.add))
                p.add("vector", lambda v: v.reduce_max(m1[:, 2:3], eq[:], AX.X))
                p.add("vector", lambda v: v.tensor_scalar(
                    eq[:], rbuf[:], colap(m1, 4, 2), None, ALU.is_ge))
                p.inc("vector", "dve", lambda v, ti=ti: v.tensor_mul(
                    wk_all[:, ti * NE:(ti + 1) * NE], rbuf[:], eq[:]))
                p.wait("tensor", "dve")
                p.add("tensor", lambda t, ti=ti: t.transpose(
                    pwt[0:8, 0:128], wk_all[:, ti * NE:(ti + 1) * NE], idf[:, :]))
                p.pe_mark()
                p.wait("vector", "pe")
                p.inc("vector", "dve", lambda v, ti=ti: v.tensor_copy(
                    wTb[0:8, ti * 128:(ti + 1) * 128], pwt[0:8, 0:128]))
            p.end_scope()

            acc = p.sb("acc", (128, (NTS // 2) * D), F32)
            weT = p.sb("weT", (128, NC * D), BF16)
            pb = p.ps("pbias", (128, 1024), F32)
            pe2 = p.ps("pe2", (128, 1024), F32)
            for half in range(2):
                hb = half * (NTS // 2)
                p.wait("vector", "st")
                xwar = {}
                for i2, ti in enumerate(range(hb, hb + NTS // 2)):
                    if i2 >= 2:
                        p.wait("sync", "dve", xwar[i2 - 2])
                    else:
                        p.wait("sync", "dve")
                        p.wait("sync", "act")
                    p.dma("sync", f"ldx{i2 % 2}", xbufs[i2 % 2][:],
                          x_d[row0 + ti * 128: row0 + ti * 128 + 128, :])
                    p.wait("tensor", "dve")
                    p.wait("tensor", "ldw")
                    for nn in range(2):
                        p.add("tensor", lambda t, ti=ti, nn=nn: t.matmul(
                            pb[:, nn * 512:(nn + 1) * 512],
                            wTb[0:8, ti * 128:(ti + 1) * 128],
                            be_sb[0:8, nn * 512:(nn + 1) * 512],
                            start=True, stop=True))
                    p.pe_mark()
                    p.wait("vector", "pe")
                    p.wait("vector", f"ldx{i2 % 2}")
                    p.inc("vector", "dve", lambda v, i2=i2, ti=ti: v.tensor_add(
                        acc[:, i2 * D:(i2 + 1) * D], pb[:, :], xbufs[i2 % 2][:]))
                    xwar[i2] = p.at("dve")
                for e in range(NE):
                    p.wait("gpsimd", "pe")
                    p.wait("gpsimd", "dve")
                    emit_weight_transpose(p, we_d[e * D:(e + 1) * D, :], weT, idb,
                                          D, D, f"e{s_i}{half}{e}")
                    for i2, ti in enumerate(range(hb, hb + NTS // 2)):
                        p.wait("tensor", "dve")
                        for nn in range(2):
                            for c in range(NC):
                                p.add("tensor", lambda t, c=c, ti=ti, nn=nn: t.matmul(
                                    pe2[:, nn * 512:(nn + 1) * 512],
                                    xnT[:, c * SEQ + ti * 128: c * SEQ + ti * 128 + 128],
                                    weT[:, c * D + nn * 512: c * D + nn * 512 + 512],
                                    start=(c == 0), stop=(c == NC - 1)))
                        p.pe_mark()
                        p.wait("vector", "pe")
                        p.inc("vector", "dve", lambda v, i2=i2, ti=ti, e=e: v.scalar_tensor_tensor(
                            acc[:, i2 * D:(i2 + 1) * D], pe2[:, :],
                            colap(wk_all, NTS * NE, ti * NE + e),
                            acc[:, i2 * D:(i2 + 1) * D], ALU.mult, ALU.add))
                for i2, ti in enumerate(range(hb, hb + NTS // 2)):
                    p.wait("gpsimd", "dve")
                    p.dma("gpsimd", "st",
                          xo_d[row0 + ti * 128: row0 + ti * 128 + 128, :],
                          acc[:, i2 * D:(i2 + 1) * D])
            p.end_scope()

        p.wait("gpsimd", "st")

    if mega is None:
        with es, nc.Block() as block:
            p.scope(); body(); p.end_scope()
            p.finalize(block)
        return nc
    hard_barrier(p)
    p.scope(); body(); p.end_scope()
    return None


_MOE_CACHE = {}


def run_moe(x_full, inputs, cores=None, seqs=BPC, qt=NTS):
    key = (seqs, qt)
    if key not in _MOE_CACHE:
        _MOE_CACHE[key] = build_moe(seqs, qt)
    nc = _MOE_CACHE[key]
    shards = _shard_x(x_full)
    base = dict(
        wex=np.ascontiguousarray(inputs["L1_ffn_W_experts"], np.float32).reshape(8 * D, D),
        bex=np.ascontiguousarray(inputs["L1_ffn_b_experts"], np.float32),
        wrt=np.ascontiguousarray(inputs["L1_ffn_W_router"], np.float32),
        brt=np.ascontiguousarray(inputs["L1_ffn_b_router"], np.float32).reshape(1, 8),
        identb=_CONSTS["IDENT_BF"],
        identf=np.eye(128, dtype=np.float32),
        onesb=_CONSTS["ONES_BF"],
    )
    cores = cores or list(range(NCORES))
    in_maps = [dict(base, x=s) for s in shards[:len(cores)]]
    res = run_bass_kernel_spmd(nc, in_maps, core_ids=cores)
    return np.stack([np.asarray(r["xo"]) for r in res.results])



def emit_norm_transpose2(p, x_dram, sc_cols, bi_cols, ncols, aw, ab, dst_fmT,
                         xbufs, xn_bf, idb, pstr, n_tiles, row_base=0,
                         seq_cols=SEQ):
    pe_done = 0
    act_hist = []
    p.wait("tensor", "ldc")
    for i in range(n_tiles):
        buf = xbufs[i % 2]
        if i >= 2:
            p.wait("sync", "act", act_hist[i - 2])
        else:
            p.wait("sync", "dve")
            p.wait("sync", "act")
        lane = f"ldx{i % 2}"
        p.dma("sync", lane, buf[:], x_dram[row_base + i * 128: row_base + (i + 1) * 128, :])
        p.wait("scalar", lane)
        if i >= 1:
            p.wait("scalar", "pe", pe_done)
        p.inc("scalar", "act",
              lambda s, b=buf, i=i: s.activation(xn_bf[:], b[:], AF.Identity,
                                                 bias=colap(bi_cols, ncols, i),
                                                 scale=colap(sc_cols, ncols, i)))
        act_hist.append(p.at("act"))
        p.wait("tensor", "act")
        if i >= 1:
            p.wait("tensor", "act" if aw is not None else "dve", ev_done)
        for cc in range(NC):
            p.add("tensor", lambda t, cc=cc: t.transpose(
                pstr[:, cc * 128:(cc + 1) * 128], xn_bf[:, cc * 128:(cc + 1) * 128], idb[:]))
        p.pe_mark()
        pe_done = p.at("pe")
        if aw is not None:
            p.wait("scalar", "pe")
            for cc in range(NC):
                p.inc("scalar", "act", lambda s, cc=cc, i=i: s.activation(
                    dst_fmT[:, cc * seq_cols + i * 128: cc * seq_cols + i * 128 + 128],
                    pstr[:, cc * 128:(cc + 1) * 128], AF.Identity,
                    bias=colap(ab, NC, cc), scale=colap(aw, NC, cc)))
            ev_done = p.at("act")
        else:
            p.wait("vector", "pe")
            dst3 = dst_fmT.ap().rearrange("q (c t) -> q c t", c=NC)
            p.inc("vector", "dve",
                  lambda v, i=i, dst3=dst3: v.tensor_copy(
                      dst3[:, :, i * 128:(i + 1) * 128], pstr[:, :]))
            ev_done = p.at("dve")


def _dram_l1a(nc, pref=""):
    mk = _mk_dram(nc, pref)
    d = dict(qw=mk("qw", [D, D]), kw=mk("kw", [D, D]), vw=mk("vw", [D, D]),
             ow=mk("ow", [D, D]), qb=mk("qb", [128, NC]), kb=mk("kb", [128, NC]),
             vb=mk("vb", [128, NC]), ob=mk("ob", [1, D]),
             l1w=mk("l1w", [128, NC]), l1b=mk("l1b", [128, NC]),
             l2w=mk("l2w", [128, NC]), l2b=mk("l2b", [128, NC]),
             fcw=mk("fcw", [4 * D, D]), fcb=mk("fcb", [128, 4 * NC]),
             pjw=mk("pjw", [D, 4 * D]), pjb=mk("pjb", [128, NC]),
             identb=mk("identb", [128, 128], BF16), onesb=mk("onesb", [1, 128], BF16))
    d["x1scr"] = nc.dram_tensor(pref + "x1scr", [TPC, D], F32).ap()
    d["qdscr"] = nc.dram_tensor(pref + "qdscr", [D, SEQ], BF16).ap()
    d["kdscr"] = nc.dram_tensor(pref + "kdscr", [D, SEQ], BF16).ap()
    d["vdscr"] = nc.dram_tensor(pref + "vdscr", [D, SEQ], BF16).ap()
    d["qkvdscr"] = nc.dram_tensor(pref + "qkvdscr", [SEQ, D], BF16).ap()
    d["qkdscr"] = nc.dram_tensor(pref + "qkdscr", [SEQ, H], F32).ap()
    return d


def build_l1a(seqs=BPC, qt=NTS, mega=None):
    if mega is None:
        nc = bass.Bass("TRN2", target_bir_lowering=False)
        es = ExitStack()
        p = Prog(nc, es)
        dd = _dram_l1a(nc)
        x_d = nc.dram_tensor("x", [TPC, D], F32, kind="ExternalInput").ap()
        xo_d = nc.dram_tensor("xo", [TPC, D], F32, kind="ExternalOutput").ap()
    else:
        p, dd, x_d, xo_d = mega
        nc = p.nc
    qw_d = dd["qw"]; kw_d = dd["kw"]; vw_d = dd["vw"]; ow_d = dd["ow"]
    qb_d = dd["qb"]; kb_d = dd["kb"]; vb_d = dd["vb"]; ob_d = dd["ob"]
    l1w_d = dd["l1w"]; l1b_d = dd["l1b"]; l2w_d = dd["l2w"]; l2b_d = dd["l2b"]
    fcw_d = dd["fcw"]; fcb_d = dd["fcb"]; pjw_d = dd["pjw"]; pjb_d = dd["pjb"]
    idb_d = dd["identb"]; onesb_d = dd["onesb"]
    x1_d = dd["x1scr"]; qd = dd["qdscr"]; kd = dd["kdscr"]; vd = dd["vdscr"]
    qkvd = dd["qkvdscr"]; qkd = dd["qkdscr"]

    def body():
        idb = p.sb("idb", (128, 128), BF16)
        onesb = p.sb("onesb_s", (1, 128), BF16)
        xb0 = p.sb("xb0", (128, D), F32)
        xb1 = p.sb("xb1", (128, D), F32)
        xbufs = (xb0, xb1)
        xn_bf = p.sb("xnbf", (128, D), BF16)
        qb_sb = p.sb("qbsb", (128, NC), F32)
        kb_sb = p.sb("kbsb", (128, NC), F32)
        vb_sb = p.sb("vbsb", (128, NC), F32)
        qb1 = p.sb("qb1sb", (128, NC), F32)
        kb1 = p.sb("kb1sb", (128, NC), F32)
        ob_sb = p.sb("obsb", (1, D), BF16)
        l1w = p.sb("l1wsb", (128, NC), F32)
        l1b_ = p.sb("l1bsb", (128, NC), F32)
        l2w = p.sb("l2wsb", (128, NC), F32)
        l2b_ = p.sb("l2bsb", (128, NC), F32)
        fcb = p.sb("fcbsb", (128, 4 * NC), F32)
        pjb = p.sb("pjbsb", (128, NC), F32)

        for sbuf, dram in ((idb, idb_d), (onesb, onesb_d), (qb_sb, qb_d),
                           (kb_sb, kb_d), (vb_sb, vb_d), (l1w, l1w_d),
                           (l1b_, l1b_d), (l2w, l2w_d), (l2b_, l2b_d),
                           (fcb, fcb_d), (pjb, pjb_d)):
            p.dma("sync", "ldc", sbuf[:], dram)
        p.dma("gpsimd", "ldw", ob_sb[:], ob_d)
        p.wait("vector", "ldc")
        p.add("vector", lambda v: v.tensor_scalar_add(qb1[:], qb_sb[:], 1.0))
        p.inc("vector", "dve", lambda v: v.tensor_scalar_add(kb1[:], kb_sb[:], 1.0))

        for s_i in range(seqs):
            row0 = s_i * SEQ
            if s_i > 0:
                for eng in ("vector", "gpsimd", "sync", "scalar", "tensor"):
                    p.wait(eng, "st")
                    p.wait(eng, "st2")
                    p.wait(eng, "st3")
                p.wait("gpsimd", "pe")
                p.wait("gpsimd", "dve")
                p.wait("gpsimd", "act")
                p.wait("sync", "pe")
                p.wait("sync", "dve")
            p.scope()
            st = ln_stats_tensors(p, f"s{s_i}", NTS)
            g1 = p.sb("g1", (128, NTS), F32)
            n1 = p.sb("n1", (128, NTS), F32)

            emit_ln_stats(p, x_d, st, xbufs, xn_bf, NTS, row_base=row0)
            p.add("vector", lambda v: v.tensor_mul(g1[:, :NTS], st["rstd"][:, :NTS], st["rstd"][:, :NTS]))
            p.inc("vector", "dve", lambda v: v.tensor_scalar(
                g1[:, :NTS], g1[:, :NTS], -1e-5, 1.0 + 1e-5, ALU.mult, ALU.add))
            p.wait("scalar", "dve")
            p.inc("scalar", "act", lambda s: s.activation(g1[:, :NTS], g1[:, :NTS], AF.Sqrt))
            p.wait("vector", "act")
            p.add("vector", lambda v: v.reciprocal(g1[:, :NTS], g1[:, :NTS]))
            p.add("vector", lambda v: v.tensor_mul(n1[:, :NTS], st["nmr"][:, :NTS], g1[:, :NTS]))
            p.inc("vector", "dve", lambda v: v.tensor_mul(g1[:, :NTS], st["rstd"][:, :NTS], g1[:, :NTS]))

            p.scope()
            x1nT = p.sb("x1nT", (128, NC * SEQ), BF16)
            pstr = p.ps("pstrA", (128, 1024), BF16)
            emit_norm_transpose2(p, x_d, g1, n1, NTS, l1w, l1b_, x1nT, xbufs,
                                 xn_bf, idb, pstr, NTS, row_base=row0)
            qwT = p.sb("qwT", (128, NC * D), BF16)
            kwT = p.sb("kwT", (128, NC * D), BF16)
            vwT = p.sb("vwT", (128, NC * D), BF16)
            p.wait("gpsimd", "pe")
            p.wait("gpsimd", "dve")
            p.wait("gpsimd", "act")
            emit_weight_transpose(p, qw_d, qwT, idb, D, D, f"q{s_i}")
            emit_weight_transpose(p, kw_d, kwT, idb, D, D, f"k{s_i}")
            emit_weight_transpose(p, vw_d, vwT, idb, D, D, f"v{s_i}")
            pq = (p.ps("pqA0", (128, 512), F32), p.ps("pqA1", (128, 512), F32))
            escr = p.sb("escr", (128, 512), F32)
            fbuf = p.sb("fbuf", (128, 512), BF16)
            p.wait("tensor", "dve")
            p.wait("tensor", "act")

            def elu_evac(b0cols, b1cols, dram):
                def ev(p, pb, fc, tg):
                    p.wait("scalar", "pe")
                    p.wait("scalar", "st2")
                    p.inc("scalar", "act", lambda s, pb=pb, fc=fc: s.activation(
                        escr[:], pb[:, :], AF.Exp, bias=colap(b0cols, NC, fc)))
                    p.wait("vector", "act")
                    p.wait("vector", "st2")
                    p.inc("vector", "dve", lambda v, pb=pb, fc=fc: v.scalar_tensor_tensor(
                        fbuf[:], pb[:, :], colap(b1cols, NC, fc), escr[:],
                        ALU.add, ALU.min))
                    p.wait("gpsimd", "dve")
                    p.dma("gpsimd", "st2", dram[fc * 128:(fc + 1) * 128,
                                                tg * 512:(tg + 1) * 512], fbuf[:])
                return ev

            emit_fm_proj(p, qwT, D, x1nT, None, pq, evac=elu_evac(qb_sb, qb1, qd))
            emit_fm_proj(p, kwT, D, x1nT, None, pq, evac=elu_evac(kb_sb, kb1, kd))

            def v_evac(p, pb, fc, tg):
                p.wait("vector", "st2")
                p.inc("vector", "dve", lambda v, pb=pb, fc=fc: v.tensor_scalar_add(
                    fbuf[:], pb[:, :], colap(vb_sb, NC, fc)))
                p.wait("gpsimd", "dve")
                p.dma("gpsimd", "st2", vd[fc * 128:(fc + 1) * 128,
                                          tg * 512:(tg + 1) * 512], fbuf[:])
            emit_fm_proj(p, vwT, D, x1nT, None, pq, evac=v_evac)
            p.end_scope()

            p.scope()
            TG = 128
            qalt = p.sb("qalt", (64, H * TG), BF16)
            kalt = p.sb("kalt", (64, H * TG), BF16)
            valt = p.sb("valt", (16, HD * TG), BF16)
            asb = p.sb("asb", (16, 512), BF16)
            qkrow = p.sb("qkrow", (1, 512), F32)
            qvbuf = p.sb("qvbuf", (16, 512), BF16)
            onesf16 = p.sb("onesf16", (16, 1), BF16)
            pa = p.ps("paA", (16, 512), F32)
            pqv = p.ps("pqvA", (16, 512), F32)
            pqk = p.ps("pqkA", (1, 512), F32)
            p.add("gpsimd", lambda g: g.memset(onesf16[:], 1.0))
            p.wait("sync", "st2")
            p.wait("sync", "pool")
            for g_i in range(SEQ // TG):
                t0 = g_i * TG
                lane = f"lda{g_i % 2}"
                if g_i >= 1:
                    p.wait("sync", "pe", pewar)
                p.dma("sync", lane,
                      qalt.ap().rearrange("d (h t) -> d h t", h=H),
                      qd.rearrange("(h d) t -> d h t", d=HD)[:, :, t0:t0 + TG])
                p.dma("sync", lane,
                      kalt.ap().rearrange("d (h t) -> d h t", h=H),
                      kd.rearrange("(h d) t -> d h t", d=HD)[:, :, t0:t0 + TG])
                p.dma("sync", lane,
                      valt.ap().rearrange("h (e t) -> h e t", e=HD),
                      vd.rearrange("(h e) t -> h e t", e=HD)[:, :, t0:t0 + TG])
                p.wait("tensor", lane)
                p.wait("tensor", "pool")
                for half in range(TG // 32):
                    p.wait("tensor", "dve")
                    for j in range(32):
                        tok = half * 32 + j
                        p.add("tensor", lambda t, tok=tok, j=j: t.matmul(
                            pa[0:16, j * 16:(j + 1) * 16],
                            bass.AP(kalt, tok, [[H * TG, 64], [TG, H]]),
                            bass.AP(qalt, tok, [[H * TG, 64], [TG, H]]),
                            start=True, stop=True))
                    p.pe_mark()
                    p.wait("vector", "pe")
                    p.inc("vector", "dve", lambda v: v.tensor_copy(asb[:, :], pa[0:16, :]))
                    p.wait("tensor", "dve")
                    p.add("tensor", lambda t: t.matmul(
                        pqk[0:1, :], onesf16[0:16, 0:1], asb[0:16, :],
                        start=True, stop=True))
                    p.pe_mark()
                    p.wait("vector", "pe")
                    p.inc("vector", "dve", lambda v: v.tensor_copy(qkrow[:], pqk[0:1, :]))
                    p.wait("gpsimd", "dve")
                    p.dma("gpsimd", "st3",
                          qkd[t0 + half * 32: t0 + half * 32 + 32, :]
                          .rearrange("t h -> () t h"),
                          qkrow.ap().rearrange("o (t h) -> o t h", h=H))
                    p.wait("vector", "st3")
                    for oct_ in range(4):
                        p.wait("tensor", "dve")
                        for j in range(8):
                            tok = half * 32 + oct_ * 8 + j
                            p.add("tensor", lambda t, tok=tok, j=j: t.matmul(
                                pqv[0:16, j * 64:(j + 1) * 64],
                                asb[:, (tok % 32) * 16:(tok % 32) * 16 + 16],
                                bass.AP(valt, tok, [[HD * TG, 16], [TG, HD]]),
                                start=True, stop=True))
                        p.pe_mark()
                        p.wait("vector", "pe")
                        p.wait("vector", "st3")
                        p.inc("vector", "dve", lambda v: v.tensor_copy(
                            qvbuf[:, :], pqv[0:16, :]))
                        p.wait("gpsimd", "dve")
                        tok0 = t0 + half * 32 + oct_ * 8
                        p.dma("gpsimd", "st3",
                              qkvd[tok0:tok0 + 8, :].rearrange("t (h e) -> h t e", h=H),
                              qvbuf.ap().rearrange("q (t e) -> q t e", t=8))
                pewar = p.at("pe")
            p.end_scope()

            p.scope()
            attnT = p.sb("attnT", (128, NC * SEQ), BF16)
            owT = p.sb("owT", (128, NC * D), BF16)
            p.wait("gpsimd", "pe")
            p.wait("gpsimd", "dve")
            p.wait("gpsimd", "act")
            emit_weight_transpose(p, ow_d, owT, idb, D, D, f"o{s_i}")
            p.wait("sync", "ldw")
            p.wait("sync", "pe")
            p.wait("sync", "dve")
            qkvb = p.sb("qkvb", (128, D), BF16)
            qkt = p.sb("qkt", (128, H), F32)
            atm = p.sb("atm", (128, D), BF16)
            xnf = p.sb("xnf", (128, D), F32)
            x1buf = p.sb("x1buf", (128, D), F32)
            pstr3 = p.ps("pstr3A", (128, 1024), BF16)
            po = (p.ps("poA0", (128, 512), F32), p.ps("poA1", (128, 512), F32))
            p.wait("sync", "st3")
            p.wait("tensor", "dve")
            p.wait("tensor", "act")
            evdo = {}
            for ti in range(NTS):
                lane = "ldv0"
                if ti >= 1:
                    p.wait("sync", "dve", qk_done)
                p.dma("sync", lane, qkvb[:], qkvd[ti * 128:(ti + 1) * 128, :])
                p.dma("sync", lane, qkt[:], qkd[ti * 128:(ti + 1) * 128, :])
                p.wait("vector", lane)
                p.add("vector", lambda v: v.reciprocal(qkt[:], qkt[:]))
                p.inc("vector", "dve", lambda v: v.tensor_mul(
                    atm[:], qkvb[:], bass.AP(qkt, 0, [[H, 128], [1, H], [0, HD]])))
                qk_done = p.at("dve")
                p.wait("tensor", "dve")
                for cc in range(NC):
                    p.add("tensor", lambda t, cc=cc: t.transpose(
                        pstr3[:, cc * 128:(cc + 1) * 128],
                        atm[:, cc * 128:(cc + 1) * 128], idb[:]))
                p.pe_mark()
                p.wait("vector", "pe")
                dst3 = attnT.ap().rearrange("q (c t) -> q c t", c=NC)
                p.inc("vector", "dve", lambda v, ti=ti, dst3=dst3: v.tensor_copy(
                    dst3[:, :, ti * 128:(ti + 1) * 128], pstr3[:, :]))
            for ti in range(NTS):
                lane = f"ldx{ti % 2}"
                if ti >= 2:
                    p.wait("sync", "act", xn_done)
                else:
                    p.wait("sync", "act")
                    p.wait("sync", "dve")
                p.dma("sync", lane, xbufs[ti % 2][:],
                      x_d[row0 + ti * 128: row0 + ti * 128 + 128, :])
                p.wait("scalar", lane)
                p.wait("scalar", "st")
                p.inc("scalar", "act", lambda s, ti=ti: s.activation(
                    xnf[:], xbufs[ti % 2][:], AF.Identity,
                    bias=colap(st["nmr"], NTS, ti), scale=colap(st["rstd"], NTS, ti)))
                xn_done = p.at("act")
                for nn in range(2):
                    pb = po[nn]
                    if evdo.get(nn):
                        p.wait("tensor", "dve", evdo[nn])
                    p.add("tensor", lambda t, pb=pb, nn=nn: t.matmul(
                        pb[:], onesb[0:1, :], ob_sb[0:1, nn * 512: nn * 512 + 512],
                        start=True, stop=False))
                    for c in range(NC):
                        p.add("tensor", lambda t, c=c, ti=ti, nn=nn, pb=pb: t.matmul(
                            pb[:], attnT[:, c * SEQ + ti * 128: c * SEQ + ti * 128 + 128],
                            owT[:, c * D + nn * 512: c * D + nn * 512 + 512],
                            start=False, stop=(c == NC - 1)))
                    p.pe_mark()
                    p.wait("vector", "pe")
                    p.wait("vector", "act")
                    p.wait("vector", "st")
                    p.inc("vector", "dve", lambda v, pb=pb, nn=nn: v.tensor_add(
                        x1buf[:, nn * 512:(nn + 1) * 512], pb[:],
                        xnf[:, nn * 512:(nn + 1) * 512]))
                    evdo[nn] = p.at("dve")
                p.wait("gpsimd", "dve")
                p.dma("gpsimd", "st", x1_d[row0 + ti * 128: row0 + ti * 128 + 128, :],
                      x1buf[:])
            p.end_scope()

            p.scope()
            st2 = ln_stats_tensors(p, f"t{s_i}", NTS)
            x2nT = p.sb("x2nT", (128, NC * SEQ), BF16)
            pstr4 = p.ps("pstr4A", (128, 1024), BF16)
            p.wait("sync", "st")
            p.wait("vector", "st")
            p.wait("scalar", "st")
            emit_ln_stats(p, x1_d, st2, xbufs, xn_bf, NTS, row_base=row0)
            emit_norm_transpose2(p, x1_d, st2["rstd"], st2["nmr"], NTS, l2w, l2b_,
                                 x2nT, xbufs, xn_bf, idb, pstr4, NTS, row_base=row0)
            hT = p.sb("hT", (128, 32 * 512), BF16)
            fcT = p.sb("fcT", (128, NC * 128), BF16)
            pjT = p.sb("pjT", (128, 32 * 128), BF16)
            wtmp2 = p.sb("wtmp2", (128, 4 * D), BF16)
            mlpT = p.sb("mlpT", (128, NC * 512), BF16)
            x1b2 = p.sb("x1b2", (128, D), F32)
            outb = p.sb("outb", (128, D), F32)
            psh = p.ps("pshA", (128, 512), F32)
            psw = p.ps("pswA", (128, 1024), BF16)
            pstm = p.ps("pstmA", (128, 1024), BF16)
            p.wait("tensor", "act")
            p.wait("tensor", "dve")
            for tg in range(4):
                tc0 = tg * 512
                for oc in range(32):
                    p.wait("gpsimd", "pe")
                    p.dma("gpsimd", "ldw", wtmp2[:, 0:D],
                          fcw_d[oc * 128:(oc + 1) * 128, :])
                    p.wait("tensor", "ldw")
                    p.wait("tensor", "act")
                    for cc in range(NC):
                        p.add("tensor", lambda t, cc=cc: t.transpose(
                            psw[:, cc * 128:(cc + 1) * 128],
                            wtmp2[:, cc * 128:(cc + 1) * 128], idb[:]))
                    p.pe_mark()
                    p.wait("vector", "pe")
                    p.inc("vector", "dve", lambda v: v.tensor_copy(fcT[:, :], psw[:, :]))
                    p.wait("tensor", "dve")
                    for c in range(NC):
                        p.add("tensor", lambda t, c=c, tg=tg: t.matmul(
                            psh[:, :], fcT[:, c * 128:(c + 1) * 128],
                            x2nT[:, c * SEQ + tg * 512: c * SEQ + tg * 512 + 512],
                            start=(c == 0), stop=(c == NC - 1)))
                    p.pe_mark()
                    p.wait("scalar", "pe")
                    p.inc("scalar", "act", lambda s, oc=oc: s.activation(
                        hT[:, oc * 512:(oc + 1) * 512], psh[:, :], AF.Gelu,
                        bias=colap(fcb, 4 * NC, oc)))
                for oc in range(NC):
                    p.wait("gpsimd", "pe")
                    p.dma("gpsimd", "ldw", wtmp2[:, :],
                          pjw_d[oc * 128:(oc + 1) * 128, :])
                    p.wait("tensor", "ldw")
                    p.wait("tensor", "act")
                    p.wait("tensor", "dve")
                    for g8 in range(4):
                        for gi in range(8):
                            fc2 = g8 * 8 + gi
                            p.add("tensor", lambda t, fc2=fc2, gi=gi: t.transpose(
                                psw[:, gi * 128:(gi + 1) * 128],
                                wtmp2[:, fc2 * 128:(fc2 + 1) * 128], idb[:]))
                        p.pe_mark()
                        p.wait("vector", "pe")
                        p.inc("vector", "dve", lambda v, g8=g8: v.tensor_copy(
                            pjT[:, g8 * 1024:(g8 + 1) * 1024], psw[:, :]))
                        p.wait("tensor", "dve")
                    for fc2 in range(32):
                        p.add("tensor", lambda t, fc2=fc2, tg=tg: t.matmul(
                            psh[:, :], pjT[:, fc2 * 128:(fc2 + 1) * 128],
                            hT[:, fc2 * 512:(fc2 + 1) * 512],
                            start=(fc2 == 0), stop=(fc2 == 31)))
                    p.pe_mark()
                    p.wait("vector", "pe")
                    p.inc("vector", "dve", lambda v, oc=oc: v.tensor_scalar_add(
                        mlpT[:, oc * 512:(oc + 1) * 512], psh[:, :],
                        colap(pjb, NC, oc)))
                for i2 in range(4):
                    ti = tg * 4 + i2
                    p.wait("tensor", "dve")
                    for cc in range(NC):
                        p.add("tensor", lambda t, cc=cc, i2=i2: t.transpose(
                            pstm[:, cc * 128:(cc + 1) * 128],
                            mlpT[:, cc * 512 + i2 * 128: cc * 512 + i2 * 128 + 128],
                            idb[:]))
                    p.pe_mark()
                    lane = f"ldf{i2 % 2}"
                    p.wait("sync", "dve")
                    p.dma("sync", lane, xbufs[i2 % 2][:],
                          x_d[row0 + ti * 128: row0 + ti * 128 + 128, :])
                    p.dma("sync", lane, x1b2[:],
                          x1_d[row0 + ti * 128: row0 + ti * 128 + 128, :])
                    p.wait("vector", lane)
                    p.wait("vector", "pe")
                    p.wait("vector", "st")
                    mtm = bass.AP(pstm, 0, [[1024, 128], [128, NC], [1, 128]])
                    p.add("vector", lambda v, i2=i2, mtm=mtm: v.tensor_add(
                        outb.ap().rearrange("q (c t) -> q c t", c=NC), mtm,
                        xbufs[i2 % 2].ap().rearrange("q (c t) -> q c t", c=NC)))
                    p.inc("vector", "dve", lambda v: v.tensor_add(
                        outb[:], outb[:], x1b2[:]))
                    p.wait("gpsimd", "dve")
                    p.dma("gpsimd", "st", xo_d[row0 + ti * 128: row0 + ti * 128 + 128, :],
                          outb[:])
            p.end_scope()
            p.end_scope()
        for sem in ("st", "st2", "st3"):
            p.wait("gpsimd", sem)

    if mega is None:
        with es, nc.Block() as block:
            p.scope(); body(); p.end_scope()
            p.finalize(block)
        return nc
    hard_barrier(p)
    p.scope(); body(); p.end_scope()
    return None


_L1A_CACHE = {}


def run_l1a(x_full, inputs, cores=None, seqs=BPC, qt=NTS):
    key = (seqs, qt)
    if key not in _L1A_CACHE:
        _L1A_CACHE[key] = build_l1a(seqs, qt)
    nc = _L1A_CACHE[key]
    shards = _shard_x(x_full)

    def cmaj(vec):
        return np.ascontiguousarray(np.asarray(vec, np.float32).reshape(-1, 128).T)

    base = dict(
        qw=np.ascontiguousarray(inputs["L1_attn_a_q_w"], np.float32),
        kw=np.ascontiguousarray(inputs["L1_attn_a_k_w"], np.float32),
        vw=np.ascontiguousarray(inputs["L1_attn_a_v_w"], np.float32),
        ow=np.ascontiguousarray(inputs["L1_attn_a_o_w"], np.float32),
        qb=cmaj(inputs["L1_attn_a_q_b"]), kb=cmaj(inputs["L1_attn_a_k_b"]),
        vb=cmaj(inputs["L1_attn_a_v_b"]),
        ob=np.ascontiguousarray(inputs["L1_attn_a_o_b"], np.float32).reshape(1, D),
        l1w=cmaj(inputs["L1_attn_a_ln1_w"]), l1b=cmaj(inputs["L1_attn_a_ln1_b"]),
        l2w=cmaj(inputs["L1_attn_a_ln2_w"]), l2b=cmaj(inputs["L1_attn_a_ln2_b"]),
        fcw=np.ascontiguousarray(inputs["L1_attn_a_fc_w"], np.float32),
        fcb=cmaj(inputs["L1_attn_a_fc_b"]),
        pjw=np.ascontiguousarray(inputs["L1_attn_a_proj_w"], np.float32),
        pjb=cmaj(inputs["L1_attn_a_proj_b"]),
        identb=_CONSTS["IDENT_BF"], onesb=_CONSTS["ONES_BF"],
    )
    cores = cores or list(range(NCORES))
    in_maps = [dict(base, x=s) for s in shards[:len(cores)]]
    res = run_bass_kernel_spmd(nc, in_maps, core_ids=cores)
    return np.stack([np.asarray(r["xo"]) for r in res.results])


PS = SEQ + 1

def _dram_ffn(nc, layers=6, pref=""):
    mk = _mk_dram(nc, pref)
    DI = 2730
    R2 = ((layers * DI + 127) // 128) * 128
    return dict(
        wqkv=mk("wqkv", [layers * 128, NC * 3 * D]), qs=mk("qs", [layers, HD]),
        ks=mk("ks", [layers, HD]), wo=mk("wo", [layers * 128, NC * D]),
        lng=mk("lng", [128, layers * NC]), lnb=mk("lnb", [128, layers * NC]),
        w1=mk("w1", [layers * 44 * 128, NC * 128]), b1=mk("b1", [128, layers * 44]),
        w2=mk("w2", [R2, D]), b2=mk("b2", [128, layers * NC]),
        xpcq=mk("xpcq", [SEQ, HD]), xpsq=mk("xpsq", [SEQ, HD]),
        xpck=mk("xpck", [SEQ, HD]), xpsk=mk("xpsk", [SEQ, HD]),
        lm0=mk("lm0", [128, 512]), lm1=mk("lm1", [128, 512]),
        lmf0=mk("lmf0", [128, 512]), lmf1=mk("lmf1", [128, 512]),
        identb=mk("identb", [128, 128], BF16), identf=mk("identf", [128, 128]),
        onesb=mk("onesb", [1, 128], BF16), onescf=mk("onescf", [128, 1]),
        onesrf=mk("onesrf", [1, 128]))


def build_ffn(seqs=1, layers=6, tap=None, mega=None):
    DI = 2730
    NCH = 22
    TOK = seqs * SEQ
    if mega is None:
        nc = bass.Bass("TRN2", target_bir_lowering=False)
        es = ExitStack()
        p = Prog(nc, es)
        dd = _dram_ffn(nc, layers)
        x_d = nc.dram_tensor("x", [TOK, D], F32, kind="ExternalInput").ap()
        xo_d = nc.dram_tensor("xo", [TOK, D], F32, kind="ExternalOutput").ap()
        ydbg_d = nc.dram_tensor("ydbg", [128, NC * seqs * PS], F32, kind="ExternalOutput").ap()
        xndbg_d = nc.dram_tensor("xndbg", [128, NC * 512], F32, kind="ExternalOutput").ap()
        gdbg_d = nc.dram_tensor("gdbg", [128, 22 * 512], F32, kind="ExternalOutput").ap()
        rdbg_d = nc.dram_tensor("rdbg", [1, 2048], F32, kind="ExternalOutput").ap()
        ysdbg_d = nc.dram_tensor("ysdbg", [128, NC * 512], F32, kind="ExternalOutput").ap()
    else:
        p, dd, x_d, xo_d = mega
        nc = p.nc
        ydbg_d = xndbg_d = gdbg_d = rdbg_d = ysdbg_d = None
    wqkv_d = dd["wqkv"]; qs_d = dd["qs"]; ks_d = dd["ks"]; wo_d = dd["wo"]
    lng_d = dd["lng"]; lnb_d = dd["lnb"]; w1_d = dd["w1"]; b1_d = dd["b1"]
    w2_d = dd["w2"]; b2_d = dd["b2"]
    precast_pairs = None
    if "wqkvb" in dd:
        wqkv_d = dd["wqkvb"]; wo_d = dd["wob"]; w1_d = dd["w1b"]; w2_d = dd["w2b"]
    else:
        R2 = ((layers * DI + 127) // 128) * 128
        twq = nc.dram_tensor("wqkvb", [layers * 128, NC * 3 * D], BF16).ap()
        two = nc.dram_tensor("wob", [layers * 128, NC * D], BF16).ap()
        tw1 = nc.dram_tensor("w1b", [layers * 44 * 128, NC * 128], BF16).ap()
        tw2 = nc.dram_tensor("w2b", [R2, D], BF16).ap()
        precast_pairs = [
            (wqkv_d, twq, layers * 128, NC * 3 * D),
            (wo_d, two, layers * 128, NC * D),
            (w1_d, tw1, layers * 44 * 128, NC * 128),
            (w2_d, tw2, R2, D)]
        wqkv_d = twq; wo_d = two; w1_d = tw1; w2_d = tw2
    cq_d = dd["xpcq"]; sq_d = dd["xpsq"]; ck_d = dd["xpck"]; sk_d = dd["xpsk"]
    lm0_d = dd["lm0"]; lm1_d = dd["lm1"]; lmf0_d = dd["lmf0"]; lmf1_d = dd["lmf1"]
    idb_d = dd["identb"]; idf_d = dd["identf"]; onesb_d = dd["onesb"]
    onescf_d = dd["onescf"]; onesrf_d = dd["onesrf"]

    def body():
        if precast_pairs is not None:
            emit_precast(p, precast_pairs)
        y = p.sb("ystream", (128, NC * seqs * PS), F32)
        idb = p.sb("idb", (128, 128), BF16)
        idf = p.sb("idf", (128, 128), F32)
        onesb = p.sb("onesb_s", (1, 128), BF16)
        onescf = p.sb("onescf_s", (128, 1), F32)
        onesrf = p.sb("onesrf_s", (1, 128), F32)
        lm = {}
        for nm, dd in (("lm0", lm0_d), ("lm1", lm1_d)):
            lm[nm] = p.sb(nm + "_s", (128, 512), F32)
            p.dma("sync", "ldc", lm[nm][:], dd)
        lm["lmf0"] = lm["lm0"]
        lm["lmf1"] = lm["lm1"]
        lng = p.sb("lng_s", (128, layers * NC), F32)
        lnb = p.sb("lnb_s", (128, layers * NC), F32)
        b1c = p.sb("b1_s", (128, layers * 44), F32)
        b2c = p.sb("b2_s", (128, layers * NC), F32)
        for sbuf, dram in ((idb, idb_d), (idf, idf_d), (onesb, onesb_d),
                           (onescf, onescf_d), (onesrf, onesrf_d),
                           (lng, lng_d), (lnb, lnb_d), (b1c, b1_d), (b2c, b2_d)):
            p.dma("sync", "ldc", sbuf[:], dram)

        xb0 = p.sb("xb0", (128, D), F32)
        xb1 = p.sb("xb1", (128, D), F32)
        xbufs = (xb0, xb1)
        xn_f = p.sb("xnf", (128, D), F32)
        sqscr = p.sb("sqscr", (128, D), BF16)

        def ycol(c, s, t):
            return c * (seqs * PS) + s * PS + 1 + t

        NTL = seqs * NTS
        p.scope()
        st = ln_stats_tensors(p, "pro", NTL)
        emit_ln_stats(p, x_d, st, xbufs, sqscr, NTL)
        pstf = p.ps("pstf", (128, 512), F32)
        pstf2 = p.ps("pstf2", (128, 512), F32)
        p.add("gpsimd", lambda g: g.memset(
            bass.AP(y, 0, [[NC * seqs * PS, 128], [seqs * PS, NC], [PS, seqs], [1, 1]]), 0.0))
        act_hist = []
        pe_done = 0
        ev_done = {}
        p.wait("tensor", "ldc")
        for i in range(NTL):
            s_idx, t_idx = divmod(i, NTS)
            buf = xbufs[i % 2]
            if i >= 2:
                p.wait("sync", "act", act_hist[i - 2])
            else:
                p.wait("sync", "dve")
                p.wait("sync", "act")
            lane = f"ldx{i % 2}"
            p.dma("sync", lane, buf[:], x_d[i * 128:(i + 1) * 128, :])
            p.wait("scalar", lane)
            if i >= 1:
                p.wait("scalar", "pe", pe_done)
            p.inc("scalar", "act", lambda s, b=buf, i=i: s.activation(
                xn_f[:], b[:], AF.Identity, bias=colap(st["nmr"], NTL, i),
                scale=colap(st["rstd"], NTL, i)))
            act_hist.append(p.at("act"))
            p.wait("tensor", "act")
            for g2 in range(2):
                pp = (pstf, pstf2)[g2]
                if ev_done.get(g2):
                    p.wait("tensor", "dve", ev_done[g2])
                for cc in range(4):
                    c = g2 * 4 + cc
                    p.add("tensor", lambda t, c=c, cc=cc, pp=pp: t.transpose(
                        pp[:, cc * 128:(cc + 1) * 128],
                        xn_f[:, c * 128:(c + 1) * 128], idf[:]))
                p.pe_mark()
                if g2 == 1:
                    pe_done = p.at("pe")
                p.wait("vector", "pe")
                p.inc("vector", "dve", lambda v, g2=g2, s_idx=s_idx, t_idx=t_idx, pp=pp:
                      v.tensor_copy(
                          bass.AP(y, ycol(g2 * 4, s_idx, t_idx * 128),
                                  [[NC * seqs * PS, 128], [seqs * PS, 4], [1, 128]]),
                          pp.ap().rearrange("q (c t) -> q c t", c=4)))
                ev_done[g2] = p.at("dve")
        p.end_scope()
        if tap == "pro":
            p.wait("gpsimd", "dve")
            p.wait("gpsimd", "pool")
            p.dma("gpsimd", "st", ydbg_d, y[:])
            for _e in ("vector", "scalar", "tensor", "gpsimd", "sync"):
                p.wait(_e, "st")
        for li in range(layers):
            emit_ffn_layer(p, li, dict(tap=(tap if li == 0 else None),
                ydbg_d=ydbg_d, xndbg_d=xndbg_d, gdbg_d=gdbg_d,
                rdbg_d=rdbg_d, ysdbg_d=ysdbg_d,
                y=y, ycol=ycol, seqs=seqs, idb=idb, idf=idf, onesb=onesb,
                onescf=onescf, onesrf=onesrf, lm=lm, lng=lng, lnb=lnb,
                b1c=b1c, b2c=b2c, wqkv_d=wqkv_d, qs_d=qs_d, ks_d=ks_d,
                wo_d=wo_d, w1_d=w1_d, w2_d=w2_d, cq_d=cq_d, sq_d=sq_d,
                ck_d=ck_d, sk_d=sk_d, layers=layers))

        p.wait("tensor", "dve")
        p.wait("tensor", "act")
        p.wait("tensor", "pool")
        pep = (p.ps("pep0", (128, 512), F32), p.ps("pep1", (128, 512), F32))
        xout = p.sb("xoutF", (128, D), F32)
        evd = {}
        st_hist = 0
        xwar = {}
        for i in range(NTL):
            s_idx, t_idx = divmod(i, NTS)
            if i >= 2:
                p.wait("sync", "dve", xwar[i - 2])
            else:
                p.wait("sync", "dve")
                p.wait("sync", "act")
            lane = f"ldx{i % 2}"
            p.dma("sync", lane, xbufs[i % 2][:], x_d[i * 128:(i + 1) * 128, :])
            for g2 in range(2):
                pp = pep[g2]
                if evd.get(g2):
                    p.wait("tensor", "dve", evd[g2])
                for cc in range(4):
                    c = g2 * 4 + cc
                    p.add("tensor", lambda t, c=c, cc=cc, pp=pp, s_idx=s_idx, t_idx=t_idx:
                          t.transpose(pp[:, cc * 128:(cc + 1) * 128],
                                      bass.AP(y, ycol(c, s_idx, t_idx * 128),
                                              [[NC * seqs * PS, 128], [1, 128]]),
                                      idf[:]))
                p.pe_mark()
                p.wait("vector", "pe")
                p.wait("vector", lane)
                if i >= 1:
                    p.wait("vector", "st", st_hist)
                p.inc("vector", "dve", lambda v, g2=g2, pp=pp, i=i: v.tensor_add(
                    xout.ap().rearrange("q (g c t) -> q g c t", g=2, c=4)[:, g2],
                    pp.ap().rearrange("q (c t) -> q c t", c=4),
                    xbufs[i % 2].ap().rearrange("q (g c t) -> q g c t", g=2, c=4)[:, g2]))
                evd[g2] = p.at("dve")
            xwar[i] = p.at("dve")
            p.wait("gpsimd", "dve")
            p.dma("gpsimd", "st", xo_d[i * 128:(i + 1) * 128, :], xout[:])
            st_hist = p.at("st")
        p.wait("gpsimd", "st")

    if mega is None:
        with es, nc.Block() as block:
            p.scope(); body(); p.end_scope()
            p.finalize(block)
        return nc
    hard_barrier(p)
    p.scope(); body(); p.end_scope()
    return None


def emit_ffn_layer(p, li, cx):
    y = cx["y"]; ycol = cx["ycol"]; seqs = cx["seqs"]
    idb = cx["idb"]; idf = cx["idf"]
    onescf = cx["onescf"]; onesrf = cx["onesrf"]
    lm = cx["lm"]; lng = cx["lng"]; lnb = cx["lnb"]
    b1c = cx["b1c"]; b2c = cx["b2c"]
    DI = 2730
    NCH = 22
    p.scope()
    qsrow = p.sb("qsrow", (1, 4 * HD), F32)
    repq = p.sb("repq", (128, 4 * HD), F32)
    p.wait("sync", "pe")
    p.wait("sync", "dve")
    p.dma("sync", "ldc", qsrow[0:1, 0:HD], cx["qs_d"][li:li + 1, :])
    p.dma("sync", "ldc", qsrow[0:1, 2 * HD:3 * HD], cx["ks_d"][li:li + 1, :])
    p.wait("vector", "ldc")
    p.add("vector", lambda v: v.tensor_copy(
        qsrow.ap()[0:1, HD:HD + 32], qsrow.ap()[0:1, 32:64]))
    p.add("vector", lambda v: v.tensor_copy(
        qsrow.ap()[0:1, HD + 32:HD + 64], qsrow.ap()[0:1, 0:32]))
    p.add("vector", lambda v: v.tensor_copy(
        qsrow.ap()[0:1, 3 * HD:3 * HD + 32], qsrow.ap()[0:1, 2 * HD + 32:2 * HD + 64]))
    p.inc("vector", "dve", lambda v: v.tensor_copy(
        qsrow.ap()[0:1, 3 * HD + 32:4 * HD], qsrow.ap()[0:1, 2 * HD:2 * HD + 32]))
    p.scope()
    prep = p.ps("prepq", (128, 4 * HD), F32)
    p.wait("tensor", "dve")
    p.wait("tensor", "ldc")
    p.add("tensor", lambda t: t.matmul(prep[:, :], onesrf[0:1, :], qsrow[0:1, :],
                                       start=True, stop=True))
    p.pe_mark()
    p.wait("vector", "pe")
    p.inc("vector", "dve", lambda v: v.tensor_copy(repq[:, :], prep[:, :]))
    p.end_scope()

    p.scope()
    wqkv = p.sb("wqkv_l", (128, NC * 3 * D), BF16)
    wo_sb = p.sb("wo_l", (128, NC * D), BF16)
    p.wait("gpsimd", "pe")
    p.wait("gpsimd", "dve")
    p.wait("gpsimd", "act")
    p.dma("gpsimd", "ldw", wqkv[:, :], cx["wqkv_d"][li * 128:(li + 1) * 128, :])
    p.dma("gpsimd", "ldw", wo_sb[:, :], cx["wo_d"][li * 128:(li + 1) * 128, :])
    ybfw = p.sb("ybfw", (128, NC * 128), BF16)
    qsq = p.sb("qsq", (128, 512), BF16)
    s2 = p.sb("s2l", (128, 8), F32)
    rn = p.sb("rnl", (128, 16), F32)
    ctab = p.sb("ctab", (128, 2 * HD), F32)
    ceff = p.sb("ceff", (128, 2 * HD), F32)
    ab1 = p.sb("ab1l", (128, 512), F32)
    ab2 = p.sb("ab2l", (128, 512), BF16)
    qtm = p.sb("qtml", (128, 512), BF16)
    RD = 6
    qTr = p.sb("qTrl", (128, NC * RD * 128), BF16)
    kTr = p.sb("kTrl", (128, NC * RD * 128), BF16)
    vr = p.sb("vrl", (128, RD * D), BF16)
    s_sb = p.sb("ssbl", (128, 512), F32)
    p2b = p.sb("p2l", (128, 512), BF16)
    p3b = p.sb("p3l", (128, 512), BF16)
    pTb = p.sb("pTl", (128, 512), BF16)
    mxl = p.sb("mxl", (128, 4), F32)
    lsl = p.sb("lsl", (128, 2), F32)
    rll = p.sb("rll", (128, 1), F32)
    oTw = p.sb("oTwl", (128, NC * 256), BF16)
    pq = [p.ps("pql0", (128, 512), F32), p.ps("pql1", (128, 512), F32)]
    ptr = p.ps("ptrl", (128, 1024), BF16)
    ps_s = [p.ps("pssl0", (128, 512), F32), p.ps("pssl1", (128, 512), F32)]
    ps_pv = [p.ps("pspvl0", (64, 512), F32), p.ps("pspvl1", (64, 512), F32)]
    pwo = p.ps("pwol", (128, 256), F32)
    p.wait("tensor", "ldw")
    p.wait("tensor", "dve")
    p.wait("tensor", "act")
    p.wait("tensor", "pool")

    for s_i in range(seqs):
        for t_i in range(NTS):
            slot = t_i % 6
            p.wait("gpsimd", "pe")
            for c in range(NC):
                off = ycol(c, s_i, t_i * 128) - (1 if c >= 4 else 0)
                p.inc("gpsimd", "pool", lambda g, c=c, off=off: g.tensor_copy(
                    ybfw[:, c * 128:(c + 1) * 128],
                    bass.AP(y, off, [[NC * seqs * PS, 128], [1, 128]])))
            for (tens, half) in (("q", 0), ("q", 1), ("k", 0), ("k", 1),
                                 ("v", 0), ("v", 1)):
                par = half
                bank = pq[par]
                sl = {"q": 0, "k": 1, "v": 2}[tens]
                p.wait("tensor", "pool")
                p.wait("tensor", "dve")
                p.wait("tensor", "act")
                for c in range(NC):
                    p.add("tensor", lambda t, c=c, sl=sl, half=half, bank=bank: t.matmul(
                        bank[:, :], ybfw[:, c * 128:(c + 1) * 128],
                        wqkv[:, c * 3 * D + sl * D + half * 512:
                             c * 3 * D + sl * D + half * 512 + 512],
                        start=(c == 0), stop=(c == NC - 1)))
                p.pe_mark()
                if tens == "v":
                    p.wait("vector", "pe")
                    p.inc("vector", "dve", lambda v, slot=slot, half=half, bank=bank:
                          v.tensor_copy(vr[:, slot * D + half * 512:
                                           slot * D + half * 512 + 512], bank[:, :]))
                    continue
                toff = 2 * HD * sl
                p.wait("scalar", "pe")
                p.wait("scalar", "dve")
                p.inc("scalar", "act", lambda s, bank=bank: s.activation(
                    qsq[:], bank[:, :], AF.Square))
                p.wait("vector", "act")
                p.add("vector", lambda v: v.tensor_reduce(
                    s2[:, :], qsq.ap().rearrange("q (h d) -> q h d", d=HD),
                    AX.X, ALU.add))
                p.wait("scalar", "dve")
                p.inc("scalar", "act", lambda s: s.activation(s2[:, :], s2[:, :], AF.Sqrt))
                p.wait("vector", "act")
                p.add("vector", lambda v: v.tensor_scalar_max(s2[:, :], s2[:, :], 1e-12))
                p.add("vector", lambda v, half=half: v.reciprocal(
                    rn[:, half * 8:half * 8 + 8], s2[:, :]))
                cdr = cx["cq_d"] if tens == "q" else cx["ck_d"]
                sdr = cx["sq_d"] if tens == "q" else cx["sk_d"]
                if half == 0:
                    p.wait("sync", "dve")
                    p.dma("sync", f"ldt{par}", ctab[:, 0:HD],
                          cdr[t_i * 128:(t_i + 1) * 128, :])
                    p.dma("sync", f"ldt{par}", ctab[:, HD:2 * HD],
                          sdr[t_i * 128:(t_i + 1) * 128, :])
                    p.wait("vector", f"ldt{par}")
                    p.add("vector", lambda v, toff=toff: v.tensor_mul(
                        ceff[:, 0:HD], ctab[:, 0:HD],
                        repq[:, toff:toff + HD]))
                    p.add("vector", lambda v, toff=toff: v.tensor_mul(
                        ceff[:, HD:2 * HD], ctab[:, HD:2 * HD],
                        repq[:, toff + HD:toff + 2 * HD]))
                ce = bass.AP(ceff, 0, [[2 * HD, 128], [0, 8], [1, HD]])
                se = bass.AP(ceff, HD, [[2 * HD, 128], [0, 8], [1, HD]])
                p.add("vector", lambda v, bank=bank, ce=ce: v.tensor_mul(
                    ab1.ap().rearrange("q (h d) -> q h d", d=HD), bank.ap()
                    .rearrange("q (h d) -> q h d", d=HD), ce))
                swap = bass.AP(bank, 32, [[512, 128], [64, 8], [-32, 2], [1, 32]])
                p.add("vector", lambda v, swap=swap, se=se: v.tensor_mul(
                    ab2.ap().rearrange("q (h e d) -> q h e d", h=8, e=2), swap,
                    se.rearrange("q h (e d) -> q h e d", e=2)))
                p.add("vector", lambda v: v.tensor_add(ab1[:, :], ab1[:, :], ab2[:, :]))
                rnap = bass.AP(rn, half * 8, [[16, 128], [1, 8], [0, HD]])
                p.inc("vector", "dve", lambda v, rnap=rnap: v.tensor_mul(
                    qtm.ap().rearrange("q (h d) -> q h d", d=HD),
                    ab1.ap().rearrange("q (h d) -> q h d", d=HD), rnap))
                p.wait("tensor", "dve")
                for cc in range(4):
                    p.add("tensor", lambda t, cc=cc: t.transpose(
                        ptr[:, cc * 128:(cc + 1) * 128],
                        qtm[:, cc * 128:(cc + 1) * 128], idb[:]))
                p.pe_mark()
                p.wait("vector", "pe")
                dstT = qTr if tens == "q" else kTr
                p.inc("vector", "dve", lambda v, dstT=dstT, half=half, slot=slot:
                      v.tensor_copy(
                          bass.AP(dstT, (half * 4) * (RD * 128) + slot * 128,
                                  [[NC * RD * 128, 128], [RD * 128, 4], [1, 128]]),
                          ptr.ap()[:, 0:512].rearrange("q (c t) -> q c t", c=4)))
                p.wait("tensor", "dve")

            wlist = []
            if t_i >= 2 and t_i % 2 == 0:
                wlist = [(t_i - 2) // 2]
            if t_i == NTS - 1:
                wlist = wlist + [NTS // 2 - 1]
            for w in wlist:
                jlo = 0 if w > 0 else 2
                for h in range(H):
                    par = h % 2
                    fc = h // 2
                    r0 = 64 * (h % 2)
                    for sub in range(2):
                        qslot = (2 * w + sub) % 6
                        sps = ps_s[par]
                        p.wait("tensor", "dve")
                        for j in range(jlo, 4):
                            kslot = (2 * w - 2 + j) % 6
                            p.add("tensor", lambda t, j=j, kslot=kslot, qslot=qslot,
                                  fc=fc, r0=r0, sps=sps: t.matmul(
                                      sps[:, j * 128:(j + 1) * 128],
                                      qTr[r0:r0 + 64, fc * (RD * 128) + qslot * 128:
                                          fc * (RD * 128) + qslot * 128 + 128],
                                      kTr[r0:r0 + 64, fc * (RD * 128) + kslot * 128:
                                          fc * (RD * 128) + kslot * 128 + 128],
                                      start=True, stop=True))
                        p.pe_mark()
                        p.wait("vector", "pe")
                        W2w = 512 - jlo * 128
                        mk = lm[("lmf" if w == 0 else "lm") + str(sub)]
                        p.add("vector", lambda v, sps=sps, mk=mk, jlo=jlo, W2w=W2w:
                              v.scalar_tensor_tensor(
                                  s_sb[:, 0:W2w], sps[:, jlo * 128:512], 8.0,
                                  mk[:, jlo * 128:512], ALU.mult, ALU.add))
                        p.inc("vector", "dve", lambda v, W2w=W2w: v.reduce_max(
                            mxl[:, 0:1], s_sb[:, 0:W2w], AX.X))
                        p.add("vector", lambda v: v.tensor_scalar_mul(
                            mxl[:, 1:2], mxl[:, 0:1], -1.0))
                        p.wait("scalar", "dve")
                        p.inc("scalar", "act", lambda s, W2w=W2w: s.activation(
                            p2b[:, 0:W2w], s_sb[:, 0:W2w], AF.Exp,
                            bias=colap(mxl, 4, 1), accum_out=lsl[:, 0:1]))
                        p.wait("vector", "act")
                        p.add("vector", lambda v: v.reciprocal(rll[:], lsl[:, 0:1]))
                        p.inc("vector", "dve", lambda v, W2w=W2w: v.tensor_scalar_mul(
                            p3b[:, 0:W2w], p2b[:, 0:W2w], colap(rll, 1, 0)))
                        nkb = W2w // 128
                        p.wait("tensor", "dve")
                        for g2 in range(nkb):
                            p.add("tensor", lambda t, g2=g2: t.transpose(
                                ptr[:, g2 * 128:(g2 + 1) * 128],
                                p3b[:, g2 * 128:(g2 + 1) * 128], idb[:]))
                        p.pe_mark()
                        p.wait("vector", "pe")
                        p.inc("vector", "dve", lambda v, nkb=nkb: v.tensor_copy(
                            pTb[:, 0:nkb * 128], ptr[:, 0:nkb * 128]))
                        p.wait("tensor", "dve")
                        spv = ps_pv[par]
                        for j in range(jlo, 4):
                            kslot = (2 * w - 2 + j) % 6
                            p.add("tensor", lambda t, j=j, jlo=jlo, kslot=kslot, h=h,
                                  spv=spv: t.matmul(
                                      spv[0:64, 0:128],
                                      vr[:, kslot * D + h * HD: kslot * D + h * HD + HD],
                                      pTb[:, (j - jlo) * 128:(j - jlo + 1) * 128],
                                      start=(j == jlo), stop=(j == 3)))
                        p.pe_mark()
                        p.wait("vector", "pe")
                        p.inc("vector", "dve", lambda v, spv=spv, fc=fc, r0=r0, sub=sub:
                              v.tensor_copy(
                                  oTw[r0:r0 + 64, fc * 256 + sub * 128:
                                      fc * 256 + sub * 128 + 128],
                                  spv[0:64, 0:128]))
                        p.wait("tensor", "dve")
                for fc in range(NC):
                    p.wait("tensor", "dve")
                    for c in range(NC):
                        p.add("tensor", lambda t, c=c, fc=fc: t.matmul(
                            pwo[:, :], wo_sb[:, c * D + fc * 128: c * D + fc * 128 + 128],
                            oTw[:, c * 256:(c + 1) * 256],
                            start=(c == 0), stop=(c == NC - 1)))
                    p.pe_mark()
                    p.wait("vector", "pe")
                    yap = bass.AP(y, ycol(fc, s_i, (2 * w) * 128),
                                  [[NC * seqs * PS, 128], [1, 256]])
                    p.inc("vector", "dve", lambda v, yap=yap: v.tensor_add(
                        yap, pwo[:, :], yap))
                    p.wait("tensor", "dve")
    p.end_scope()
    if cx.get("tap") == "attn":
        p.wait("gpsimd", "dve")
        p.wait("gpsimd", "pe")
        p.wait("gpsimd", "pool")
        p.wait("gpsimd", "act")
        p.dma("gpsimd", "st", cx["ydbg_d"], y[:])
        for _e in ("vector", "scalar", "tensor", "gpsimd", "sync"):
            p.wait(_e, "st")

    p.scope()
    w2sb = p.sb("w2_l", (128, NCH * D), BF16)
    p.wait("gpsimd", "pe")
    p.wait("gpsimd", "dve")
    p.wait("gpsimd", "act")
    for j in range(NCH):
        r0 = j * 128
        w = min(128, DI - r0)
        p.dma("gpsimd", "ldw", w2sb[0:w, j * D:(j + 1) * D],
              cx["w2_d"][li * DI + r0: li * DI + r0 + w, :])
    ysq = [p.sb("ysq0", (128, 512), F32), p.sb("ysq1", (128, 512), F32)]
    rows = p.sb("rows", (1, 4 * 512), F32)
    ysh = p.sb("ysh", (128, NC * 512), F32)
    ybnd = p.sb("ybnd", (128, 4), F32)
    xnw = p.sb("xnw", (128, NC * 512), BF16)
    wa = p.sb("wa", (128, NC * 128), BF16)
    wg = p.sb("wg", (128, NC * 128), BF16)
    gelub = p.sb("gelub", (128, 512), BF16)
    gated = p.sb("gated", (128, NCH * 512), BF16)
    pstat = [p.ps("pstat0", (1, 512), F32), p.ps("pstat1", (1, 512), F32)]
    prep2 = [p.ps("prepB0", (128, 512), F32), p.ps("prepB1", (128, 512), F32)]
    pag = [p.ps("pagB0", (128, 512), F32), p.ps("pagB1", (128, 512), F32)]
    pw2 = p.ps("pw2B", (128, 512), F32)
    DIcols = DI
    p.wait("tensor", "dve")
    p.wait("tensor", "act")
    for s_i in range(seqs):
        for tg in range(4):
            tc0 = tg * 512
            p.wait("vector", "pe")
            p.wait("vector", "act")
            for c in range(NC):
                if c < 4:
                    p.add("vector", lambda v, c=c, s_i=s_i, tc0=tc0: v.tensor_copy(
                        ysh[:, c * 512:(c + 1) * 512],
                        bass.AP(y, ycol(c, s_i, tc0), [[NC * seqs * PS, 128], [1, 512]])))
                else:
                    if tg == 0:
                        p.add("vector", lambda v, c=c, s_i=s_i: v.tensor_copy(
                            ysh[:, c * 512:c * 512 + 1],
                            bass.AP(y, ycol(c, s_i, -1), [[NC * seqs * PS, 128], [1, 1]])))
                    else:
                        p.add("vector", lambda v, c=c: v.tensor_copy(
                            ysh[:, c * 512:c * 512 + 1], ybnd[:, c - 4:c - 3]))
                    p.add("vector", lambda v, c=c, s_i=s_i, tc0=tc0: v.tensor_copy(
                        ysh[:, c * 512 + 1:(c + 1) * 512],
                        bass.AP(y, ycol(c, s_i, tc0), [[NC * seqs * PS, 128], [1, 511]])))
            p.inc("vector", "dve", lambda v, s_i=s_i, tc0=tc0: v.tensor_copy(
                ybnd[:, 0:4],
                bass.AP(y, ycol(4, s_i, tc0 + 511),
                        [[NC * seqs * PS, 128], [seqs * PS, 4], [1, 1]])))
            yslice = lambda c: ysh.ap()[:, c * 512:(c + 1) * 512]
            p.wait("tensor", "dve")
            p.wait("tensor", "act")
            p.wait("scalar", "dve")
            for c in range(NC):
                p.wait("scalar", "pe")
                p.inc("scalar", "act", lambda s, c=c: s.activation(
                    ysq[c % 2][:], yslice(c), AF.Square))
                p.wait("tensor", "act")
                p.add("tensor", lambda t, c=c: t.matmul(
                    pstat[0][0:1, :], onescf[:, 0:1], yslice(c),
                    start=(c == 0), stop=(c == NC - 1)))
                p.add("tensor", lambda t, c=c: t.matmul(
                    pstat[1][0:1, :], onescf[:, 0:1], ysq[c % 2][:],
                    start=(c == 0), stop=(c == NC - 1)))
                p.pe_mark()
            p.wait("vector", "pe")
            p.add("vector", lambda v: v.tensor_scalar_mul(
                rows.ap()[0:1, 0:512], pstat[0][0:1, :], 1.0 / D))
            p.add("vector", lambda v: v.tensor_mul(
                rows.ap()[0:1, 512:1024], rows.ap()[0:1, 0:512],
                rows.ap()[0:1, 0:512]))
            p.add("vector", lambda v: v.scalar_tensor_tensor(
                rows.ap()[0:1, 512:1024], pstat[1][0:1, :], 1.0 / D,
                rows.ap()[0:1, 512:1024], ALU.mult, ALU.subtract))
            p.inc("vector", "dve", lambda v: v.tensor_scalar_add(
                rows.ap()[0:1, 512:1024], rows.ap()[0:1, 512:1024], 1e-5))
            p.wait("scalar", "dve")
            p.inc("scalar", "act", lambda s: s.activation(
                rows.ap()[0:1, 1024:1536], rows.ap()[0:1, 512:1024], AF.Sqrt))
            p.wait("vector", "act")
            p.add("vector", lambda v: v.reciprocal(
                rows.ap()[0:1, 1024:1536], rows.ap()[0:1, 1024:1536]))
            p.add("vector", lambda v: v.tensor_mul(
                rows.ap()[0:1, 1536:2048], rows.ap()[0:1, 0:512],
                rows.ap()[0:1, 1024:1536]))
            p.inc("vector", "dve", lambda v: v.tensor_scalar_mul(
                rows.ap()[0:1, 1536:2048], rows.ap()[0:1, 1536:2048], -1.0))
            p.wait("tensor", "dve")
            p.add("tensor", lambda t: t.matmul(
                prep2[0][:, :], onesrf[0:1, :], rows.ap()[0:1, 1024:1536],
                start=True, stop=True))
            p.add("tensor", lambda t: t.matmul(
                prep2[1][:, :], onesrf[0:1, :], rows.ap()[0:1, 1536:2048],
                start=True, stop=True))
            p.pe_mark()
            p.wait("vector", "pe")
            for c in range(NC):
                p.add("vector", lambda v, c=c: v.tensor_mul(
                    ysq[0][:, :], yslice(c), prep2[0][:, :]))
                p.inc("vector", "dve", lambda v: v.tensor_add(
                    ysq[0][:, :], ysq[0][:, :], prep2[1][:, :]))
                p.wait("scalar", "dve")
                p.inc("scalar", "act", lambda s, c=c: s.activation(
                    xnw[:, c * 512:(c + 1) * 512], ysq[0][:, :], AF.Identity,
                    scale=colap(lng, cx["layers"] * NC, li * NC + c),
                    bias=colap(lnb, cx["layers"] * NC, li * NC + c)))
                p.wait("vector", "act")
            if cx.get("tap") == "ffnb" and s_i == 0 and tg == 0:
                p.wait("gpsimd", "act")
                p.wait("gpsimd", "dve")
                p.dma("gpsimd", "st", cx["xndbg_d"], xnw[:])
                p.dma("gpsimd", "st", cx["rdbg_d"], rows[0:1, :])
                p.dma("gpsimd", "st", cx["ysdbg_d"], ysh[:])
                p.dma("gpsimd", "st", cx["ydbg_d"], y[:])
            for j in range(NCH):
                r0 = j * 128
                wj = min(128, DIcols - r0)
                p.wait("gpsimd", "pe")
                ba = (li * 44 + 2 * j) * 128
                if wj == 128:
                    p.dma("gpsimd", "ldw", wa[:, :], cx["w1_d"][ba:ba + 128, :])
                    p.dma("gpsimd", "ldw", wg[:, :],
                          cx["w1_d"][ba + 128:ba + 256, :])
                else:
                    p.dma("gpsimd", "ldw",
                          wa.ap().rearrange("q (c f) -> q c f", c=NC)[:, :, 0:wj],
                          cx["w1_d"][ba:ba + 128, :]
                          .rearrange("q (c f) -> q c f", f=128)[:, :, 0:wj])
                    p.dma("gpsimd", "ldw",
                          wg.ap().rearrange("q (c f) -> q c f", c=NC)[:, :, 0:wj],
                          cx["w1_d"][ba + 128:ba + 256, :]
                          .rearrange("q (c f) -> q c f", f=128)[:, :, 0:wj])
                p.wait("tensor", "ldw")
                p.wait("tensor", "act")
                p.wait("tensor", "dve")
                for c in range(NC):
                    p.add("tensor", lambda t, c=c, wj=wj: t.matmul(
                        pag[0][0:wj, :], wa[:, c * 128:c * 128 + wj],
                        xnw[:, c * 512:(c + 1) * 512],
                        start=(c == 0), stop=(c == NC - 1)))
                    p.add("tensor", lambda t, c=c, wj=wj: t.matmul(
                        pag[1][0:wj, :], wg[:, c * 128:c * 128 + wj],
                        xnw[:, c * 512:(c + 1) * 512],
                        start=(c == 0), stop=(c == NC - 1)))
                p.pe_mark()
                p.wait("scalar", "pe")
                p.inc("scalar", "act", lambda s, wj=wj, j=j: s.activation(
                    gelub[0:wj, :], pag[1][0:wj, :], AF.Gelu,
                    bias=colap(b1c, cx["layers"] * 44, li * 44 + 2 * j + 1, nrows=wj)))
                p.wait("vector", "act")
                p.inc("vector", "dve", lambda v, wj=wj, j=j: v.scalar_tensor_tensor(
                    gated[0:wj, j * 512:(j + 1) * 512], pag[0][0:wj, :],
                    colap(b1c, cx["layers"] * 44, li * 44 + 2 * j, nrows=wj),
                    gelub[0:wj, :], ALU.add, ALU.mult))
            if cx.get("tap") == "ffnb" and s_i == 0 and tg == 0:
                p.wait("gpsimd", "dve")
                p.dma("gpsimd", "st", cx["gdbg_d"][:, 0:21 * 512], gated[:, 0:21 * 512])
                p.dma("gpsimd", "st", cx["gdbg_d"][0:42, 21 * 512:22 * 512],
                      gated[0:42, 21 * 512:22 * 512])
                for _e in ("vector", "scalar", "tensor", "sync"):
                    p.wait(_e, "st")
            for fo in range(NC):
                p.wait("tensor", "dve")
                p.wait("tensor", "act")
                for j in range(NCH):
                    wj = min(128, DIcols - j * 128)
                    p.add("tensor", lambda t, j=j, fo=fo, wj=wj: t.matmul(
                        pw2[:, :], w2sb[0:wj, j * D + fo * 128: j * D + fo * 128 + 128],
                        gated[0:wj, j * 512:(j + 1) * 512],
                        start=(j == 0), stop=(j == NCH - 1)))
                p.pe_mark()
                p.wait("vector", "pe")
                yap2 = bass.AP(y, ycol(fo, s_i, tc0),
                               [[NC * seqs * PS, 128], [1, 512]])
                p.inc("vector", "dve", lambda v, yap2=yap2, fo=fo: v.scalar_tensor_tensor(
                    yap2, pw2[:, :],
                    colap(b2c, cx["layers"] * NC, li * NC + fo),
                    yap2, ALU.add, ALU.add))
                p.wait("tensor", "dve")
    p.end_scope()
    p.end_scope()


_FFN_CACHE = {}


def run_ffn(x_full, inputs, cores=None, seqs=1, layers=6):
    key = (seqs, layers)
    if key not in _FFN_CACHE:
        _FFN_CACHE[key] = build_ffn(seqs, layers)
    nc = _FFN_CACHE[key]
    x_full = np.asarray(x_full, np.float32)
    shards = [np.ascontiguousarray(x_full[i]) for i in range(x_full.shape[0])]

    def bias_cols(vec, ncols):
        v = np.asarray(vec, np.float32).reshape(-1)
        out = np.zeros((128, ncols), np.float32)
        L = v.shape[0]
        for cc in range(ncols):
            seg = v[cc * 128:(cc + 1) * 128]
            out[:seg.shape[0], cc] = seg
        return np.ascontiguousarray(out)

    lay = 6
    b1 = np.asarray(inputs["L0_ffn_b1"], np.float32)
    DI = 2730
    b1cols = np.zeros((128, lay * 44), np.float32)
    for l in range(lay):
        for j in range(22):
            r0 = j * 128
            w = min(128, DI - r0)
            b1cols[:w, l * 44 + 2 * j] = b1[l, r0:r0 + w]
            b1cols[:w, l * 44 + 2 * j + 1] = b1[l, DI + r0: DI + r0 + w]
    lngc = np.zeros((128, lay * NC), np.float32)
    lnbc = np.zeros((128, lay * NC), np.float32)
    b2c_ = np.zeros((128, lay * NC), np.float32)
    for l in range(lay):
        lngc[:, l * NC:(l + 1) * NC] = np.asarray(inputs["L0_ffn_ln_g"][l], np.float32).reshape(NC, 128).T
        lnbc[:, l * NC:(l + 1) * NC] = np.asarray(inputs["L0_ffn_ln_b"][l], np.float32).reshape(NC, 128).T
        b2c_[:, l * NC:(l + 1) * NC] = np.asarray(inputs["L0_ffn_b2"][l], np.float32).reshape(NC, 128).T

    wqkvr, wor, w1r, w2r = _ffn_weight_relayout(inputs, layers)
    base = dict(
        wqkv=wqkvr,
        qs=np.ascontiguousarray(inputs["L0_ffn_q_scale"], np.float32)[:layers],
        ks=np.ascontiguousarray(inputs["L0_ffn_k_scale"], np.float32)[:layers],
        wo=wor,
        lng=np.ascontiguousarray(lngc[:, :layers * NC]),
        lnb=np.ascontiguousarray(lnbc[:, :layers * NC]),
        w1=w1r,
        b1=np.ascontiguousarray(b1cols[:, :layers * 44]),
        w2=w2r,
        b2=np.ascontiguousarray(b2c_[:, :layers * NC]),
        xpcq=_CONSTS["XPOS_CQ"], xpsq=_CONSTS["XPOS_SQ"],
        xpck=_CONSTS["XPOS_CK"], xpsk=_CONSTS["XPOS_SK"],
        lm0=_CONSTS["LOC_MASK0"], lm1=_CONSTS["LOC_MASK1"],
        lmf0=_CONSTS["LOC_MASKF0"], lmf1=_CONSTS["LOC_MASKF1"],
        identb=_CONSTS["IDENT_BF"], identf=np.eye(128, dtype=np.float32),
        onesb=_CONSTS["ONES_BF"],
        onescf=np.ones((128, 1), np.float32), onesrf=np.ones((1, 128), np.float32),
    )
    cores = cores or list(range(NCORES))
    in_maps = [dict(base, x=s) for s in shards[:len(cores)]]
    res = run_bass_kernel_spmd(nc, in_maps, core_ids=cores)
    return np.stack([np.asarray(r["xo"]) for r in res.results])


def build_mega():
    nc = bass.Bass("TRN2", target_bir_lowering=False)
    es = ExitStack()
    p = Prog(nc, es)

    x_d = nc.dram_tensor("x", [TPC, D], F32, kind="ExternalInput").ap()
    xo_d = nc.dram_tensor("xo", [TPC, D], F32, kind="ExternalOutput").ap()
    xsA = nc.dram_tensor("xsA", [TPC, D], F32).ap()
    xsB = nc.dram_tensor("xsB", [TPC, D], F32).ap()

    d_aa = _dram_attn_a(nc, "aa_")
    d_f0 = _dram_ffn(nc, 6, "f0_")
    d_ab = _dram_attn_bl1(nc, "b", "ab_")
    d_la = _dram_l1a(nc, "la_")
    d_mo = _dram_moe(nc, "mo_")
    d_lb = _dram_attn_bl1(nc, "l1b", "lb_")

    lay = 6
    R2 = ((lay * 2730 + 127) // 128) * 128
    d_f0["wqkvb"] = nc.dram_tensor("f0_wqkvb", [lay * 128, NC * 3 * D], BF16).ap()
    d_f0["wob"] = nc.dram_tensor("f0_wob", [lay * 128, NC * D], BF16).ap()
    d_f0["w1b"] = nc.dram_tensor("f0_w1b", [lay * 44 * 128, NC * 128], BF16).ap()
    d_f0["w2b"] = nc.dram_tensor("f0_w2b", [R2, D], BF16).ap()

    with es, nc.Block() as block:
        emit_precast(p, [
            (d_f0["wqkv"], d_f0["wqkvb"], lay * 128, NC * 3 * D),
            (d_f0["wo"], d_f0["wob"], lay * 128, NC * D),
            (d_f0["w1"], d_f0["w1b"], lay * 44 * 128, NC * 128),
            (d_f0["w2"], d_f0["w2b"], R2, D)])
        build_attn_a(mega=(p, d_aa, x_d, xsA))
        for s in range(BPC):
            build_ffn(seqs=1, layers=6,
                      mega=(p, d_f0, xsA[s * SEQ:(s + 1) * SEQ, :],
                            xsB[s * SEQ:(s + 1) * SEQ, :]))
        build_attn_bl1("b", mega=(p, d_ab, xsB, xsA))
        for s in range(BPC):
            build_ffn(seqs=1, layers=6,
                      mega=(p, d_f0, xsA[s * SEQ:(s + 1) * SEQ, :],
                            xsB[s * SEQ:(s + 1) * SEQ, :]))
        build_l1a(mega=(p, d_la, xsB, xsA))
        build_moe(mega=(p, d_mo, xsA, xsB))
        build_attn_bl1("l1b", mega=(p, d_lb, xsB, xsA))
        build_moe(mega=(p, d_mo, xsA, xo_d))
        p.finalize(block)
    return nc


def _mega_inputs(inputs):
    D_ = D

    def cmaj(vec):
        return np.ascontiguousarray(np.asarray(vec, np.float32).reshape(-1, 128).T)

    m = {}
    m.update({
        "aa_wq": np.ascontiguousarray(inputs["L0_attn_a_Wq"], np.float32),
        "aa_wk": np.ascontiguousarray(inputs["L0_attn_a_Wk"], np.float32),
        "aa_wv": np.ascontiguousarray(inputs["L0_attn_a_Wv"], np.float32),
        "aa_wo": np.ascontiguousarray(inputs["L0_attn_a_Wo"], np.float32),
        "aa_bo": np.ascontiguousarray(inputs["L0_attn_a_bo"], np.float32).reshape(1, D_),
        "aa_rel": np.ascontiguousarray(inputs["L0_attn_a_rel_emb"], np.float32).reshape(1, 17),
        "aa_adiag": _CONSTS["AA_DIAG"], "aa_amask": _CONSTS["AA_MASK"],
        "aa_identb": _CONSTS["IDENT_BF"], "aa_onesb": _CONSTS["ONES_BF"],
        "aa_eighth": _CONSTS["EIGHTH_F32"],
    })
    lay = 6
    DI = 2730
    b1 = np.asarray(inputs["L0_ffn_b1"], np.float32)
    b1cols = np.zeros((128, lay * 44), np.float32)
    for l in range(lay):
        for j in range(22):
            r0 = j * 128
            w = min(128, DI - r0)
            b1cols[:w, l * 44 + 2 * j] = b1[l, r0:r0 + w]
            b1cols[:w, l * 44 + 2 * j + 1] = b1[l, DI + r0: DI + r0 + w]
    lngc = np.zeros((128, lay * NC), np.float32)
    lnbc = np.zeros((128, lay * NC), np.float32)
    b2c_ = np.zeros((128, lay * NC), np.float32)
    for l in range(lay):
        lngc[:, l * NC:(l + 1) * NC] = np.asarray(inputs["L0_ffn_ln_g"][l], np.float32).reshape(NC, 128).T
        lnbc[:, l * NC:(l + 1) * NC] = np.asarray(inputs["L0_ffn_ln_b"][l], np.float32).reshape(NC, 128).T
        b2c_[:, l * NC:(l + 1) * NC] = np.asarray(inputs["L0_ffn_b2"][l], np.float32).reshape(NC, 128).T
    wqkvr, wor, w1r, w2r = _ffn_weight_relayout(inputs, lay)
    m.update({
        "f0_wqkv": wqkvr,
        "f0_qs": np.ascontiguousarray(inputs["L0_ffn_q_scale"], np.float32),
        "f0_ks": np.ascontiguousarray(inputs["L0_ffn_k_scale"], np.float32),
        "f0_wo": wor,
        "f0_lng": lngc, "f0_lnb": lnbc,
        "f0_w1": w1r,
        "f0_b1": b1cols,
        "f0_w2": w2r,
        "f0_b2": b2c_,
        "f0_xpcq": _CONSTS["XPOS_CQ"], "f0_xpsq": _CONSTS["XPOS_SQ"],
        "f0_xpck": _CONSTS["XPOS_CK"], "f0_xpsk": _CONSTS["XPOS_SK"],
        "f0_lm0": _CONSTS["LOC_MASK0"], "f0_lm1": _CONSTS["LOC_MASK1"],
        "f0_lmf0": _CONSTS["LOC_MASKF0"], "f0_lmf1": _CONSTS["LOC_MASKF1"],
        "f0_identb": _CONSTS["IDENT_BF"], "f0_identf": np.eye(128, dtype=np.float32),
        "f0_onesb": _CONSTS["ONES_BF"],
        "f0_onescf": np.ones((128, 1), np.float32),
        "f0_onesrf": np.ones((1, 128), np.float32),
    })
    m.update({
        "ab_wq": np.ascontiguousarray(inputs["L0_attn_b_wq"], np.float32),
        "ab_wk": np.ascontiguousarray(inputs["L0_attn_b_wk"], np.float32),
        "ab_wv": np.ascontiguousarray(inputs["L0_attn_b_wv"], np.float32),
        "ab_wp": np.ascontiguousarray(inputs["L0_attn_b_wp"], np.float32),
        "ab_bq": np.ascontiguousarray(inputs["L0_attn_b_bq"], np.float32).reshape(1, D_),
        "ab_bk": np.ascontiguousarray(inputs["L0_attn_b_bk"], np.float32).reshape(1, D_),
        "ab_bv": np.ascontiguousarray(inputs["L0_attn_b_bv"], np.float32).reshape(1, D_),
        "ab_bp": np.ascontiguousarray(inputs["L0_attn_b_bp"], np.float32).reshape(1, D_),
        "ab_ropec": _CONSTS["ROPE_COS"], "ab_ropes": _CONSTS["ROPE_SIN"],
        "ab_sccol": np.full((128, 1), 0.125, np.float32),
        "ab_amask": _CONSTS["AA_MASK"], "ab_identb": _CONSTS["IDENT_BF"],
        "ab_onesb": _CONSTS["ONES_BF"],
    })
    m.update({
        "la_qw": np.ascontiguousarray(inputs["L1_attn_a_q_w"], np.float32),
        "la_kw": np.ascontiguousarray(inputs["L1_attn_a_k_w"], np.float32),
        "la_vw": np.ascontiguousarray(inputs["L1_attn_a_v_w"], np.float32),
        "la_ow": np.ascontiguousarray(inputs["L1_attn_a_o_w"], np.float32),
        "la_qb": cmaj(inputs["L1_attn_a_q_b"]), "la_kb": cmaj(inputs["L1_attn_a_k_b"]),
        "la_vb": cmaj(inputs["L1_attn_a_v_b"]),
        "la_ob": np.ascontiguousarray(inputs["L1_attn_a_o_b"], np.float32).reshape(1, D_),
        "la_l1w": cmaj(inputs["L1_attn_a_ln1_w"]), "la_l1b": cmaj(inputs["L1_attn_a_ln1_b"]),
        "la_l2w": cmaj(inputs["L1_attn_a_ln2_w"]), "la_l2b": cmaj(inputs["L1_attn_a_ln2_b"]),
        "la_fcw": np.ascontiguousarray(inputs["L1_attn_a_fc_w"], np.float32),
        "la_fcb": cmaj(inputs["L1_attn_a_fc_b"]),
        "la_pjw": np.ascontiguousarray(inputs["L1_attn_a_proj_w"], np.float32),
        "la_pjb": cmaj(inputs["L1_attn_a_proj_b"]),
        "la_identb": _CONSTS["IDENT_BF"], "la_onesb": _CONSTS["ONES_BF"],
    })
    m.update({
        "mo_wex": np.ascontiguousarray(inputs["L1_ffn_W_experts"], np.float32).reshape(8 * D_, D_),
        "mo_bex": np.ascontiguousarray(inputs["L1_ffn_b_experts"], np.float32),
        "mo_wrt": np.ascontiguousarray(inputs["L1_ffn_W_router"], np.float32),
        "mo_brt": np.ascontiguousarray(inputs["L1_ffn_b_router"], np.float32).reshape(1, 8),
        "mo_identb": _CONSTS["IDENT_BF"], "mo_identf": np.eye(128, dtype=np.float32),
        "mo_onesb": _CONSTS["ONES_BF"],
    })
    m.update({
        "lb_wq": np.ascontiguousarray(inputs["L1_attn_b_Wq"], np.float32),
        "lb_wk": np.ascontiguousarray(inputs["L1_attn_b_Wk"], np.float32),
        "lb_wv": np.ascontiguousarray(inputs["L1_attn_b_Wv"], np.float32),
        "lb_wp": np.ascontiguousarray(inputs["L1_attn_b_Wu"], np.float32),
        "lb_bp": np.ascontiguousarray(inputs["L1_attn_b_bu"], np.float32).reshape(1, D_),
        "lb_sccol": np.full((128, 1), float(D_) ** -0.5, np.float32),
        "lb_amask": _CONSTS["AA_MASK"], "lb_identb": _CONSTS["IDENT_BF"],
        "lb_onesb": _CONSTS["ONES_BF"],
    })
    return m


_MEGA_CACHE = {}


def kernel(**inputs):
    if "nc" not in _MEGA_CACHE:
        _MEGA_CACHE["nc"] = build_mega()
    nc = _MEGA_CACHE["nc"]
    x = np.asarray(inputs["L0_attn_a_x"], np.float32)
    shards = _shard_x(x)
    base = _mega_inputs(inputs)
    in_maps = [dict(base, x=s) for s in shards]
    res = run_bass_kernel_spmd(nc, in_maps, core_ids=list(range(NCORES)))
    out = np.stack([np.asarray(r["xo"]) for r in res.results])
    return out.reshape(B, SEQ, D).astype(np.float32)
